# version 66 (final)
# baseline (speedup 1.0000x reference)
import contextlib
import numpy as np
import concourse.bass as bass
import concourse.mybir as mybir
from concourse.bass_utils import run_bass_kernel_spmd
from concourse.alu_op_type import AluOpType

F32 = mybir.dt.float32
BF16 = mybir.dt.bfloat16
I32 = mybir.dt.int32
U8 = mybir.dt.uint8
AF = mybir.ActivationFunctionType
OP = AluOpType

D = 1024
NCH = 8
R = 1024
F = 4096
FCH = 32
TCORE = 2048
TG = 256
NG = TCORE // TG
NCORES = 8
LN1024 = float(np.log(1024.0))

ENGINES = ("sync", "tensor", "vector", "scalar", "gpsimd")


class Prog:
    def __init__(self):
        self.ops = {e: [] for e in ENGINES}
        self.cnt = {}
        self.lastwait = {}

    def val(self, sem):
        return self.cnt.get(sem, 0)

    def emit(self, eng, fn, waits=(), inc=None, amt=1, drain=False):
        merged = {}
        for (s, v) in waits:
            if v > 0 and merged.get(s, -1) < v:
                merged[s] = v
        ws = list(merged.items())
        if inc is not None:
            self.cnt[inc] = self.cnt.get(inc, 0) + amt
        self.ops[eng].append((ws, fn, inc, amt, drain))
        return self.cnt.get(inc, 0) if inc is not None else None

    def replay(self, block, handles):
        for eng_name in ENGINES:
            if not self.ops[eng_name]:
                continue

            def body(eng, _ops=self.ops[eng_name]):
                for ws, fn, inc, amt, drain in _ops:
                    if drain:
                        eng.drain()
                    for (s, v) in ws:
                        eng.wait_ge(handles[s], v)
                    inst = fn(eng) if fn is not None else None
                    if inc is not None:
                        inst.then_inc(handles[inc], amt)

            getattr(block, eng_name)(body)


def build():
    nc = bass.Bass("TRN2", target_bir_lowering=False)

    xk = nc.dram_tensor("xk", [D, TCORE], F32, kind="ExternalInput")
    gT = nc.dram_tensor("gT", [D, R], F32, kind="ExternalInput")
    iwT = nc.dram_tensor("iwT", [D, D], F32, kind="ExternalInput")
    xisr = nc.dram_tensor("xisr", [1, R], F32, kind="ExternalInput")
    xispp = nc.dram_tensor("xispp", [128, NCH], F32, kind="ExternalInput")
    gampp = nc.dram_tensor("gampp", [128, NCH], F32, kind="ExternalInput")
    betpp = nc.dram_tensor("betpp", [128, NCH], F32, kind="ExternalInput")
    b2pp = nc.dram_tensor("b2pp", [128, NCH], F32, kind="ExternalInput")
    b1pp = nc.dram_tensor("b1pp", [128, FCH], F32, kind="ExternalInput")
    q1 = nc.dram_tensor("q1", [D, F], U8, kind="ExternalInput")
    q2 = nc.dram_tensor("q2", [F, D], U8, kind="ExternalInput")
    s1c = nc.dram_tensor("s1c", [8, F], F32, kind="ExternalInput")
    s2c = nc.dram_tensor("s2c", [32, D], F32, kind="ExternalInput")
    sel8 = nc.dram_tensor("sel8", [8, 1024], BF16, kind="ExternalInput")
    sel32 = nc.dram_tensor("sel32", [32, F], BF16, kind="ExternalInput")
    out = nc.dram_tensor("out", [D, TCORE], F32, kind="ExternalOutput")

    es = contextlib.ExitStack()
    sb = lambda name, shape, dt: es.enter_context(nc.sbuf_tensor(name, shape, dt))
    ps = lambda name, shape: es.enter_context(nc.psum_tensor(name, shape, F32))

    w1b = sb("w1b", [128, NCH * F], BF16)
    w2b_off = nc.sbuf_base
    w2b = sb("w2b", [128, FCH * D], BF16)
    gTb = sb("gTb", [128, NCH * R], BF16)
    WT = sb("WT", [128, NCH * D], BF16)
    NQS = 8
    xis_pp = sb("xis_pp", [128, NCH], F32)
    bias_f = sb("bias_f", [128, NCH], F32)
    bias_w = sb("bias_w", [128, NCH], F32)
    gam = sb("gam", [128, NCH], F32)
    bet = sb("bet", [128, NCH], F32)
    b2h = sb("b2h", [128, NCH], F32)
    b1c = sb("b1c", [128, FCH], F32)
    ones = sb("ones", [128, 128], BF16)
    eps_t = sb("eps_t", [1, 1], F32)
    xs_off = (nc.sbuf_base + 31) // 32 * 32
    xs = sb("xs", [128, NCH * TG], F32)
    pool_off = (nc.sbuf_base + 31) // 32 * 32
    pool = sb("pool", [128, 3 * NCH * TG], BF16)
    hb_off = nc.sbuf_base
    h_b = sb("h_b", [128, NCH * TG], BF16)
    fey = sb("fey", [128, 4 * TG], BF16)
    scr_off = nc.sbuf_base
    scr = sb("scr", [128, NCH * TG], BF16)
    NQS_ = 10
    qst = nc.alloc_sbuf_tensor_at("qst", [128, 6 * 1024], U8, offset=hb_off)
    qst3 = nc.alloc_sbuf_tensor_at("qst3", [128, 4 * 1024], U8, offset=pool_off + 8192)

    def qslot_ap(j):
        if j < 6:
            return qst[:, j * 1024:(j + 1) * 1024]
        return qst3[:, (j - 6) * 1024:(j - 5) * 1024]
    ab_row = sb("ab_row", [1, R], BF16)
    rwsb = sb("rwsb", [1, 2 * TG], F32)
    ry = sb("ry", [1, TG], F32)
    rowsA = {"nmu": sb("nmu_bA", [1, TG], BF16), "rh": sb("rb_hA", [1, TG], BF16),
             "rl": sb("rb_lA", [1, TG], BF16)}
    rowsB = {"nmu": sb("nmu_bB", [1, TG], BF16), "rh": sb("rb_hB", [1, TG], BF16),
             "rl": sb("rb_lB", [1, TG], BF16)}
    delta_b = sb("delta_b", [1, TG], BF16)
    onesc = sb("onesc", [128, 1], F32)
    dmy = sb("dmy", [1, 1], F32)

    sem_names = ["s_wp", "s_st", "s_st1", "s_x", "s_x0", "s_out", "s_dq",
                 "s_pe", "s_act", "s_dve", "s_gp"] + [f"s_q{j}" for j in range(12)]
    handles = {}
    for s in sem_names:
        handles[s] = es.enter_context(nc.semaphore(s))

    with contextlib.ExitStack() as pes:
        assert pool_off == xs_off + 8192
        iwTb = nc.alloc_sbuf_tensor_at("iwTb", [128, NCH * 1024], BF16, offset=xs_off)
        stagef = nc.alloc_sbuf_tensor_at("stagef", [128, 1024], F32, offset=w2b_off)
        stagef2 = nc.alloc_sbuf_tensor_at("stagef2", [128, 1024], F32, offset=w2b_off + 4096)
        sqs = nc.alloc_sbuf_tensor_at("sqs", [128, 1024], BF16, offset=w2b_off + 8192)
        sqs2 = nc.alloc_sbuf_tensor_at("sqs2", [128, 1024], BF16, offset=w2b_off + 10240)
        s1f = nc.alloc_sbuf_tensor_at("s1f", [8, F], F32, offset=w2b_off + 12288)
        s1b = nc.alloc_sbuf_tensor_at("s1b", [8, F], BF16, offset=w2b_off + 28672)
        s2f = nc.alloc_sbuf_tensor_at("s2f", [32, D], F32, offset=w2b_off + 36864)
        sel8b = nc.alloc_sbuf_tensor_at("sel8b", [8, 1024], BF16, offset=w2b_off + 40960)
        dw_row = nc.alloc_sbuf_tensor_at("dw_row", [1, D], BF16, offset=scr_off + 2048)
        sel32b = pes.enter_context(nc.sbuf_tensor("sel32b", [32, F], BF16))
        s2b = nc.alloc_sbuf_tensor_at("s2b", [32, D], BF16, offset=scr_off)
        pW = pes.enter_context(nc.psum_tensor("pW", [128, 1024], F32))
        ppst_t = pes.enter_context(nc.psum_tensor("ppst", [128, 512], F32))
        ppst2_t = pes.enter_context(nc.psum_tensor("ppst2", [128, 512], F32))
        ppst = ppst_t[0:1, :]
        ppst2 = ppst2_t[0:1, :]
        pw_slots = [pW[:, 0:512], pW[:, 512:1024], ppst_t[:, :], ppst2_t[:, :]]
        psc = pes.enter_context(nc.psum_tensor("psc", [128, 2048], F32))

        P = Prog()
        P.emit("sync", lambda e: e.dma_start(stagef[0:1, 0:R], xisr.ap()),
               inc="s_x", amt=16)
        for (dst, src) in [(xis_pp, xispp), (gam, gampp),
                           (bet, betpp), (b2h, b2pp), (b1c, b1pp),
                           (s1f, s1c), (s2f, s2c), (sel8b, sel8),
                           (sel32b, sel32)]:
            P.emit("sync", lambda e, d=dst, s=src: e.dma_start(d[:, :], s.ap()),
                   inc="s_wp", amt=16)
        vecs_loaded = P.val("s_wp")

        P.emit("vector", lambda e: e.memset(ones[:, :], 1.0))
        P.emit("vector", lambda e: e.memset(eps_t[:, :], 1e-5))
        P.emit("vector", lambda e: e.memset(onesc[:, :], 1.0))
        P.emit("vector", lambda e: e.memset(dmy[:, :], 1.0))
        P.emit("vector", lambda e: e.tensor_copy(ab_row[:, :], stagef[0:1, 0:R]),
               waits=[("s_x", 16)], inc="s_dve")
        ab_done = P.val("s_dve")

        stage_bufs = [stagef, stagef2]
        stage_conv = {0: 0, 1: 0}

        def stage_eighth(src, qi, dst_b, k):
            b = k % 2
            sem = "s_st" if b == 0 else "s_st1"
            w = [("s_act", stage_conv[b])] + ([("s_dve", ab_done)] if k == 0 else [])
            P.emit("sync" if b == 0 else "scalar",
                   lambda e, s=src, h=qi, bb=b: e.dma_start(
                       stage_bufs[bb][:, :],
                       s.ap()[h * 128:(h + 1) * 128, :]),
                   waits=w, inc=sem, amt=16)
            dmas = P.val(sem)
            P.emit("scalar",
                   lambda e, d=dst_b, h=qi, bb=b: e.activation(
                       d[:, h * 1024:(h + 1) * 1024], stage_bufs[bb][:, :],
                       AF.Copy, scale=1.0),
                   waits=[(sem, dmas)], inc="s_act")
            stage_conv[b] = P.val("s_act")
            return stage_conv[b]

        P.emit("vector", lambda e: e.tensor_copy(s1b[:, :], s1f[:, :]),
               waits=[("s_wp", vecs_loaded)], inc="s_dve")
        P.emit("vector", lambda e: e.tensor_copy(s2b[:, :], s2f[:, :]),
               inc="s_dve")
        scales_conv = P.val("s_dve")
        iw_conv = {}
        gt_conv = {}

        P.emit("vector", lambda e: e.tensor_tensor(bias_f[:, :], xis_pp[:, :], xis_pp[:, :], op=OP.mult),
               waits=[("s_wp", vecs_loaded)])
        P.emit("vector", lambda e: e.tensor_scalar(bias_f[:, :], bias_f[:, :], -512.0, None, op0=OP.mult),
               drain=True)
        P.emit("vector", lambda e: e.tensor_scalar(bias_w[:, :], xis_pp[:, :], 2.0, -LN1024, op0=OP.mult, op1=OP.add),
               waits=[("s_wp", vecs_loaded)])
        P.emit("vector", lambda e: e.tensor_scalar(b2h[:, :], b2h[:, :], 0.5, None, op0=OP.mult),
               waits=[("s_wp", vecs_loaded)], drain=True, inc="s_dve")
        vprep = P.val("s_dve")

        def dq_macro(m):
            if m < 32:
                fq2, hc = divmod(m, 8)
                qsrc = q1.ap()[hc * 128:(hc + 1) * 128, fq2 * 1024:(fq2 + 1) * 1024]
                lhsT = sel8b[0:8, hc * 128:(hc + 1) * 128]
                rhs0 = s1b[0:8, fq2 * 1024: fq2 * 1024 + 512]
                rhs1 = s1b[0:8, fq2 * 1024 + 512: (fq2 + 1) * 1024]
                dst = w1b[:, hc * F + fq2 * 1024: hc * F + (fq2 + 1) * 1024]
            else:
                fc = m - 32
                qsrc = q2.ap()[fc * 128:(fc + 1) * 128, :]
                lhsT = sel32b[0:32, fc * 128:(fc + 1) * 128]
                rhs0 = s2b[0:32, 0:512]
                rhs1 = s2b[0:32, 512:1024]
                dst = w2b[:, fc * D:(fc + 1) * D]
            return qsrc, lhsT, rhs0, rhs1, dst

        last_user = {}

        def emit_dq_dma(m):
            qsl = m % NQS_
            qq = "gpsimd" if qsl < 4 else ("scalar" if qsl < 7 else "sync")
            qsrc, lhsT, rhs0, rhs1, dst = dq_macro(m)
            prev = last_user.get(qsl)
            last_user[qsl] = m
            w = [("s_dq", prev + 1)] if prev is not None else []
            if m >= 32:
                w += [("s_st", 128), ("s_st1", 128)]
            P.emit(qq,
                   lambda e, q=qsrc, sl=qsl: e.dma_start(qslot_ap(sl), q),
                   waits=w, inc=f"s_q{qsl}", amt=16)
            return qsl, P.val(f"s_q{qsl}"), lhsT, rhs0, rhs1, dst

        w2_gate = {"s_act": 0, "s_pe": 0}

        def emit_dq_compute(m, issued):
            qsl, qv, lhsT, rhs0, rhs1, dst = issued
            pair = m % 2
            P.emit("tensor",
                   lambda e, l=lhsT, r=rhs0, p=pair: e.matmul(
                       psc[:, p * 1024: p * 1024 + 512], l, r, start=True, stop=True),
                   waits=[("s_dve", scales_conv), ("s_dq", m - 1)])
            P.emit("tensor",
                   lambda e, l=lhsT, r=rhs1, p=pair: e.matmul(
                       psc[:, p * 1024 + 512:(p + 1) * 1024], l, r,
                       start=True, stop=True), inc="s_pe")
            bc_done = P.val("s_pe")
            w_stt = [(f"s_q{qsl}", qv), ("s_pe", bc_done)]
            if m >= 32:
                w_stt += [("s_act", w2_gate["s_act"]), ("s_pe", w2_gate["s_pe"])]
            P.emit("vector",
                   lambda e, d=dst, sl=qsl, p=pair: e.scalar_tensor_tensor(
                       d, qslot_ap(sl), -128.0,
                       psc[:, p * 1024:(p + 1) * 1024], op0=OP.add, op1=OP.mult),
                   waits=w_stt, inc="s_dq")

        dq_issued = {}
        dq_next = [0]

        def needs_lookahead(m):
            return m >= NQS_

        def emit_dq(n):
            for _ in range(n):
                m = dq_next[0]
                if m >= 64:
                    return
                dq_next[0] = m + 1
                if m not in dq_issued:
                    dq_issued[m] = emit_dq_dma(m)
                m2 = m + 7
                if (m2 < 64 and needs_lookahead(m2) and m2 not in dq_issued
                        and (m2 < 32 or m >= 32)):
                    dq_issued[m2] = emit_dq_dma(m2)
                emit_dq_compute(m, dq_issued[m])

        for m in range(NQS_):
            dq_issued[m] = emit_dq_dma(m)


        for dc in range(NCH):
            iw_conv[dc] = stage_eighth(iwT, dc, iwTb, k=dc)
            for oh in range(2):
                sl = (dc * 2 + oh) % 4
                w = ([("s_pe", P.val("s_pe"))] if dc >= 2 else []) + \
                    [("s_act", iw_conv[dc])]
                P.emit("gpsimd",
                       lambda e, d=dc, o=oh, s=sl: e.tensor_tensor(
                           sqs[:, s * 512:(s + 1) * 512] if s < 2 else
                           sqs2[:, (s - 2) * 512:(s - 1) * 512],
                           iwTb[:, d * 1024 + o * 512: d * 1024 + (o + 1) * 512],
                           iwTb[:, d * 1024 + o * 512: d * 1024 + (o + 1) * 512],
                           op=OP.mult),
                       waits=w, drain=True, inc="s_gp")
                sq_done = P.val("s_gp")
                P.emit("tensor",
                       lambda e, d=dc, o=oh, s=sl: e.matmul(
                           (ppst if o == 0 else ppst2)[0:1, 0:512], ones[:, 0:1],
                           sqs[:, s * 512:(s + 1) * 512] if s < 2 else
                           sqs2[:, (s - 2) * 512:(s - 1) * 512],
                           start=(d == 0), stop=(d == NCH - 1),
                           skip_group_check=(0 < d < NCH - 1)),
                       waits=[("s_gp", sq_done), ("s_dve", vprep)], inc="s_pe")
            if dc == NCH - 1:
                sqn_done = P.val("s_pe")
            emit_dq(4)
        for qi in range(8):
            gt_conv[qi] = stage_eighth(gT, qi, gTb, k=8 + qi)
        w2_gate["s_act"] = max(stage_conv[0], stage_conv[1])
        w2_gate["s_pe"] = sqn_done
        for oh in range(2):
            P.emit("scalar",
                   lambda e, o=oh: e.activation(
                       dw_row[0:1, o * 512:(o + 1) * 512],
                       (ppst if o == 0 else ppst2)[0:1, 0:512],
                       AF.Copy, scale=-0.5),
                   waits=[("s_pe", sqn_done)], inc="s_act")
        dw_done = P.val("s_act")

        evac_vals = {}

        def wseries_tail(idx, mm_done_inc=True):
            rc, oh = divmod(idx, 2)
            slot_i = idx % 4
            P.emit("tensor",
                   lambda e, r=rc, o=oh, s=slot_i: e.matmul(
                       pw_slots[s],
                       ab_row[0:1, r * 128:(r + 1) * 128],
                       dw_row[0:1, o * 512:(o + 1) * 512],
                       start=False, stop=False, skip_group_check=True),
                   inc="s_pe")
            mm_done = P.val("s_pe")
            P.emit("scalar",
                   lambda e, r=rc, o=oh, s=slot_i: e.activation(
                       WT[:, r * D + o * 512: r * D + (o + 1) * 512],
                       pw_slots[s], AF.Exp,
                       bias=bias_w[:, r:r + 1], scale=xis_pp[:, r:r + 1]),
                   waits=[("s_pe", mm_done)], inc="s_act")
            evac_vals[idx] = P.val("s_act")
            return mm_done

        for dc in range(NCH):
            for idx in range(4):
                rc, oh = divmod(idx, 2)
                w_dc = [("s_act", gt_conv[dc])]
                if dc == 0:
                    w_dc.append(("s_act", dw_done))
                P.emit("tensor",
                       lambda e, r=rc, o=oh, d=dc, s=idx: e.matmul(
                           pw_slots[s],
                           gTb[:, d * R + r * 128: d * R + (r + 1) * 128],
                           iwTb[:, d * 1024 + o * 512: d * 1024 + (o + 1) * 512],
                           start=(d == 0), stop=(d == NCH - 1),
                           skip_group_check=(0 < d < NCH - 1)),
                       waits=w_dc)
        for idx in range(4):
            wseries_tail(idx)
            emit_dq(1)
        for idx in range(4, 16):
            rc, oh = divmod(idx, 2)
            slot_i = idx % 4
            waits = [("s_act", evac_vals[idx - 4])]
            for dc in range(NCH):
                w_dc = (waits if dc == 0 else []) + [("s_act", gt_conv[dc])]
                P.emit("tensor",
                       lambda e, r=rc, o=oh, d=dc, s=slot_i: e.matmul(
                           pw_slots[s],
                           gTb[:, d * R + r * 128: d * R + (r + 1) * 128],
                           iwTb[:, d * 1024 + o * 512: d * 1024 + (o + 1) * 512],
                           start=(d == 0), stop=(d == NCH - 1),
                           skip_group_check=(0 < d < NCH - 1)),
                       waits=w_dc)
            mm_done = P.val("s_pe")
            mm_done = wseries_tail(idx)
            if idx == 15:
                P.emit("scalar", lambda e: e.dma_start(
                    xs[:, :].rearrange("p (c t) -> p c t", c=NCH),
                    xk.ap()[:, 0:TG].rearrange("(c p) t -> p c t", p=128)),
                    waits=[("s_pe", mm_done)], inc="s_x0", amt=16)
            emit_dq(2)

        wt_done = P.val("s_act")
        emit_dq(64)

        P.emit("scalar", lambda e: e.activation(
            pool[:, 0:2048], xs[:, :], AF.Copy, scale=1.0),
            waits=[("s_x0", 16)], inc="s_act")
        xb0_done = P.val("s_act")

        blockP = pes.enter_context(nc.Block("prologue"))
        P.replay(blockP, handles)


    pzacc = ps("pzacc", [128, NCH * TG])
    pa = ps("pa", [128, 512])
    pb = ps("pb", [128, 512])
    pst_t = ps("pst", [128, 512])
    pstA = pst_t[0:1, :]
    pstB = pst_t[32:33, :]
    pbc = ps("pbc", [128, 512])

    M = Prog()
    M.cnt = dict(P.cnt)

    def preload6(e):
        return e.add_instruction(mybir.InstLoadActFuncSet(
            name=e.bass.get_next_instruction_name(), act_func_set_id=6,
            ins=[], outs=[]))

    def slot(i):
        return pool[:, (i % 3) * 2048:(i % 3) * 2048 + 2048]

    def slot_sl(i, dc):
        base = (i % 3) * 2048
        return pool[:, base + dc * TG: base + (dc + 1) * TG]

    st = {
        "y1_evac": {}, "x1_done": {}, "h_done": {}, "h_chunks": {},
        "y2_chunks": {}, "zstt_done": {},
        "y2_done": {}, "out_done": {}, "fc1_done": {}, "fc2_done": {},
        "attn_done": {}, "colsq_done": {}, "delta_done": {}, "ln3_rows": {},
        "pstA_free": [("s_act", 0)], "pstB_free": [("s_act", 0)],
        "pbc_free": [("s_dve", 0)], "rowsA_war": 0, "rowsB_war": 0,
        "pst_wr": 0, "pst_rd": {"s_dve": 0, "s_act": 0},
    }
    xb_done = {0: ("s_act", xb0_done)}
    x_loaded = {0: 16, 1: 32, 2: 48}

    def x_load(g, wait_out_count, extra=()):
        w = ([("s_out", 16 * wait_out_count)] if wait_out_count else []) + list(extra)
        M.emit("sync",
               lambda e, gg=g: e.dma_start(
                   xs[:, :].rearrange("p (c t) -> p c t", c=NCH),
                   xk.ap()[:, gg * TG:(gg + 1) * TG].rearrange("(c p) t -> p c t", p=128)),
               waits=w, inc="s_x", amt=16)
        x_loaded[g] = M.val("s_x")

    def xb_convert(g, extra_waits=()):
        w = [("s_x", x_loaded[g])] + list(extra_waits)
        M.emit("vector",
               lambda e, gg=g: e.tensor_copy(slot(gg), xs[:, :]),
               waits=w + [("s_dve", M.val("s_dve"))], inc="s_dve")
        xb_done[g] = ("s_dve", M.val("s_dve"))

    def emit_colsq_sq(g, scr_free_pe, on_act=False):
        w = [xb_done[g], ("s_pe", scr_free_pe)]
        if on_act:
            M.emit("scalar",
                   lambda e, gg=g: e.activation(scr[:, :], slot(gg), AF.Square,
                                                scale=1.0), waits=w, inc="s_act")
            st["colsq_sq"] = ("s_act", M.val("s_act"))
            return
        M.emit("gpsimd",
               lambda e, gg=g: e.tensor_tensor(scr[:, :], slot(gg), slot(gg),
                                               op=OP.mult),
               waits=w, drain=True, inc="s_gp")
        st["colsq_sq"] = ("s_gp", M.val("s_gp"))

    def emit_colsq_stats(g):
        pe0 = stats_mms(lambda d: scr[:, d * TG:(d + 1) * TG],
                        [st["colsq_sq"]], pstA, "pstA_free")
        st["colsq_done"][g] = pe0
        M.emit("scalar", lambda e: e.activation(delta_b[0:1, :], pstA[0:1, 0:TG], AF.Copy,
                                                scale=-0.5, bias=512.0),
               waits=[("s_pe", pe0)], inc="s_act")
        st["delta_done"][g] = M.val("s_act")
        st["pstA_free"] = [("s_act", st["delta_done"][g])]
        st["pst_rd"]["s_act"] = st["delta_done"][g]

    def dv(fn, waits=()):
        M.emit("vector", fn,
               waits=list(waits) + [("s_dve", M.val("s_dve"))], inc="s_dve")

    D_ = float(D)

    def row_front_A(rs, s_mm, war_pe):
        dv(lambda e: e.tensor_scalar(rwsb[0:1, :], pstA[0:1, :], 1.0 / D_, None,
                                     op0=OP.mult),
           waits=[("s_pe", max(s_mm, st["pst_wr"])), ("s_act", M.val("s_act"))])
        st["pstA_free"] = [("s_dve", M.val("s_dve"))]
        st["pst_rd"]["s_dve"] = M.val("s_dve")
        dv(lambda e: e.tensor_tensor(ry[0:1, :], rwsb[0:1, 0:TG], rwsb[0:1, 0:TG],
                                     op=OP.mult))
        dv(lambda e: e.scalar_tensor_tensor(rwsb[0:1, TG:2 * TG], rwsb[0:1, TG:2 * TG],
                                            1e-5, ry[0:1, :],
                                            op0=OP.add, op1=OP.subtract))
        dv(lambda e: e.tensor_copy(rs["nmu"][0:1, :], rwsb[0:1, 0:TG]),
           waits=[("s_pe", war_pe)])
        return M.val("s_dve")

    def row_front_B(rs, s_mm, war_pe):
        dv(lambda e: e.tensor_scalar(rwsb[0:1, 0:TG], pstB[0:1, 0:TG], 1.0 / D_, None,
                                     op0=OP.mult),
           waits=[("s_pe", max(s_mm, st["pst_wr"])), ("s_act", M.val("s_act"))])
        dv(lambda e: e.tensor_copy(rs["nmu"][0:1, :], rwsb[0:1, 0:TG]),
           waits=[("s_pe", war_pe)])
        dv(lambda e: e.tensor_tensor(rwsb[0:1, 0:TG], rwsb[0:1, 0:TG], rwsb[0:1, 0:TG],
                                     op=OP.mult))
        dv(lambda e: e.tensor_scalar(ry[0:1, :], pstB[0:1, TG:2 * TG], 1.0 / D_, None,
                                     op0=OP.mult))
        st["pstB_free"] = [("s_dve", M.val("s_dve"))]
        st["pst_rd"]["s_dve"] = M.val("s_dve")
        dv(lambda e: e.scalar_tensor_tensor(ry[0:1, :], ry[0:1, :], 1e-5,
                                            rwsb[0:1, 0:TG],
                                            op0=OP.add, op1=OP.subtract))
        return M.val("s_dve")

    def row_front_V(rs, s_mm):
        dv(lambda e: e.tensor_scalar(ry[0:1, :], pstB[0:1, TG:2 * TG], 1.0 / D_,
                                     1e-5, op0=OP.mult, op1=OP.add),
           waits=[("s_pe", max(s_mm, st["pst_wr"])), ("s_act", M.val("s_act"))])
        st["pstB_free"] = [("s_dve", M.val("s_dve"))]
        st["pst_rd"]["s_dve"] = M.val("s_dve")
        return M.val("s_dve")

    def row_tail(rs, var_ap, widen_ap, front_val):
        M.emit("scalar", lambda e: e.activation(var_ap, var_ap, AF.Ln, scale=1.0),
               waits=[("s_dve", front_val), ("s_act", M.val("s_act"))], inc="s_act")
        M.emit("scalar", lambda e: e.activation(var_ap, var_ap, AF.Exp, scale=-0.5),
               waits=[("s_act", M.val("s_act"))], inc="s_act")
        dv(lambda e: e.tensor_copy(rs["rh"][0:1, :], var_ap),
           waits=[("s_act", M.val("s_act"))])
        dv(lambda e: e.tensor_copy(widen_ap, rs["rh"][0:1, :]))
        dv(lambda e: e.tensor_tensor(widen_ap, var_ap, widen_ap, op=OP.subtract))
        dv(lambda e: e.tensor_copy(rs["rl"][0:1, :], widen_ap))
        return M.val("s_dve")

    varA = rwsb[0:1, TG:2 * TG]
    varB = ry[0:1, :]
    widen = rwsb[0:1, 0:TG]

    def stats_mms(src_slices, wait_list, pslot, pkey):
        for dc in range(NCH):
            w = (wait_list + st[pkey] + list(st["pst_rd"].items())) if dc == 0 else ()
            M.emit("tensor",
                   lambda e, d=dc: e.matmul(pslot[0:1, 0:TG], ones[:, 0:1],
                                            src_slices(d),
                                            start=(d == 0), stop=(d == NCH - 1)),
                   waits=w, inc="s_pe" if dc == NCH - 1 else None)
        st["pst_wr"] = M.val("s_pe")
        return M.val("s_pe")

    def stats2_mms(src_slices, wait_list, pslot):
        for dc in range(NCH):
            w = (wait_list + list(st["pst_rd"].items())) if dc == 0 else ()
            M.emit("tensor",
                   lambda e, d=dc: e.matmul(pslot[0:1, TG:2 * TG], ones[:, 0:1],
                                            src_slices(d),
                                            start=False, stop=(d == NCH - 1),
                                            skip_group_check=True),
                   waits=w, inc="s_pe" if dc == NCH - 1 else None)
        st["pst_wr"] = M.val("s_pe")
        return M.val("s_pe")

    def stats2a_mms(src_slices, wait_list, pslot):
        for dc in range(NCH):
            w = (wait_list + st["pstB_free"]
                 + list(st["pst_rd"].items())) if dc == 0 else ()
            M.emit("tensor",
                   lambda e, d=dc: e.matmul(pslot[0:1, TG:2 * TG], ones[:, 0:1],
                                            src_slices(d),
                                            start=(d == 0), stop=(d == NCH - 1)),
                   waits=w, inc="s_pe" if dc == NCH - 1 else None)
        st["pst_wr"] = M.val("s_pe")
        return M.val("s_pe")

    def bc2V_mms(rows_ready, rs):
        M.emit("tensor", lambda e: e.matmul(pbc[:, TG:2 * TG], ones[0:1, 0:128], rs["rh"][0:1, :],
                                            start=True, stop=True),
               waits=[("s_dve", rows_ready)] + st["pbc_free"])
        M.emit("tensor", lambda e: e.matmul(pbc[:, TG:2 * TG], ones[0:1, 0:128], rs["rl"][0:1, :],
                                            start=False, stop=False, skip_group_check=True),
               inc="s_pe")
        return M.val("s_pe")

    def bc_mms(rows_ready, rs):
        M.emit("tensor", lambda e: e.matmul(pbc[:, 0:TG], ones[0:1, 0:128], rs["nmu"][0:1, :],
                                            start=True, stop=True),
               waits=[("s_dve", rows_ready)] + st["pbc_free"])
        M.emit("tensor", lambda e: e.matmul(pbc[:, TG:2 * TG], ones[0:1, 0:128], rs["rh"][0:1, :],
                                            start=False, stop=False, skip_group_check=True))
        M.emit("tensor", lambda e: e.matmul(pbc[:, TG:2 * TG], ones[0:1, 0:128], rs["rl"][0:1, :],
                                            start=False, stop=False, skip_group_check=True),
               inc="s_pe")
        return M.val("s_pe")

    def emit_X(g, wu=None):
        delta_done = st["delta_done"][g]
        fe_evac = {}

        def emit_mains(rc):
            bank = rc % 2
            pwork = pa if bank == 0 else pb
            silu_free = st.get("silu_pa" if bank == 0 else "silu_pb", 0)
            w = [xb_done[g], ("s_act", silu_free)]
            if rc >= 2:
                w = [("s_act", fe_evac[rc - 2])]
            for dc in range(NCH):
                M.emit("tensor",
                       lambda e, r=rc, d=dc, p=pwork: e.matmul(
                           p[:, 0:TG],
                           gTb[:, d * R + r * 128: d * R + (r + 1) * 128],
                           slot_sl(g, d),
                           start=(d == 0), stop=False),
                       waits=w if dc == 0 else ())

        def emit_corr_evac(rc, delta_done):
            bank = rc % 2
            pwork = pa if bank == 0 else pb
            M.emit("tensor",
                   lambda e, r=rc, p=pwork: e.matmul(
                       p[:, 0:TG], ab_row[0:1, r * 128:(r + 1) * 128], delta_b[0:1, :],
                       start=False, stop=True),
                   waits=[("s_act", delta_done)], inc="s_pe")
            mm_done = M.val("s_pe")
            M.emit("scalar",
                   lambda e, r=rc, p=pwork: e.activation(
                       fey[:, (r % 2) * TG:(r % 2 + 1) * TG], p[:, 0:TG], AF.Exp,
                       bias=bias_f[:, r:r + 1], scale=xis_pp[:, r:r + 1]),
                   waits=[("s_pe", mm_done)], inc="s_act")
            fe_evac[rc] = M.val("s_act")

        def emit_attn(rc):
            w = [("s_act", fe_evac[rc])]
            if rc == 0:
                w.append(("s_dve", max(st["zstt_done"].get(g - 2, 0),
                                       st["y1_evac"].get(g - 1, 0))))
            for oc in range(NCH):
                first = (rc == 0 and oc % 2 == 0)
                last = (rc == NCH - 1 and oc % 2 == 1)
                M.emit("tensor",
                       lambda e, r=rc, o=oc, fi=first, la=last: e.matmul(
                           pzacc[:, o * TG:(o + 1) * TG],
                           WT[:, r * D + o * 128: r * D + (o + 1) * 128],
                           fey[:, (r % 2) * TG:(r % 2 + 1) * TG],
                           start=fi, stop=la,
                           skip_group_check=not (fi or la)),
                       waits=w if oc == 0 else (),
                       inc="s_pe" if (rc == NCH - 1 and oc == NCH - 1) else None)

        emit_mains(0)
        emit_mains(1)
        emit_corr_evac(0, delta_done)
        emit_corr_evac(1, delta_done)
        for rc in range(NCH):
            emit_attn(rc)
            if rc + 2 < NCH:
                emit_mains(rc + 2)
                emit_corr_evac(rc + 2, delta_done)
            if rc == 1:
                consume_defer_h()
            if wu and rc in wu:
                wu[rc]()
        st["attn_done"][g] = M.val("s_pe")
        if wu and 8 in wu:
            wu[8]()

        for oc in range(NCH):
            w = [("s_pe", st["attn_done"][g])] if oc == 0 else ()
            M.emit("vector",
                   lambda e, o=oc, gg=g: e.tensor_tensor(
                       slot_sl(gg, o), slot_sl(gg, o), pzacc[:, o * TG:(o + 1) * TG],
                       op=OP.add),
                   waits=w, drain=(oc == 0 or oc == NCH - 1),
                   inc="s_dve" if oc == NCH - 1 else None)
        st["y1_evac"][g] = M.val("s_dve")
        if g >= 2:
            M.emit("scalar", lambda e: e.activation(dmy[0:1, 0:1], dmy[0:1, 0:1],
                                                    AF.Silu, scale=1.0))

    def emit_LN3_copy(og, war=None, fine=False):
        if og < 0:
            return
        warv = war if war is not None else M.val("s_pe")
        if not fine:
            w = [("s_pe", warv), ("s_gp", st["y2_done"][og])]
            M.emit("scalar",
                   lambda e: e.activation(scr[:, :], xs[:, :], AF.Copy,
                                          scale=1.0), waits=w, inc="s_act")
            st["ln3_cp"] = ("s_act", M.val("s_act"))
            return
        qs = []
        for q in range(4):
            w = [("s_gp", st["y2_chunks"][og][2 * q + 1])]
            if q == 0:
                w.append(("s_pe", warv))
            M.emit("scalar",
                   lambda e, qq=q: e.activation(
                       scr[:, qq * 2 * TG:(qq + 1) * 2 * TG],
                       xs[:, qq * 2 * TG:(qq + 1) * 2 * TG],
                       AF.Copy, scale=1.0), waits=w, inc="s_act")
            qs.append(M.val("s_act"))
        st["ln3_cp_q"] = qs
        st["ln3_cp"] = ("s_act", qs[-1])

    def emit_LN3_s1(og, fine=False):
        if og < 0:
            return
        s1ch = []
        for dc in range(NCH):
            w = (st["pstA_free"] + list(st["pst_rd"].items())) if dc == 0 else []
            if fine:
                if dc % 2 == 0:
                    w = w + [("s_act", st["ln3_cp_q"][dc // 2])]
            elif dc == 0:
                w = w + [st["ln3_cp"]]
            M.emit("tensor",
                   lambda e, d=dc: e.matmul(pstA[0:1, 0:TG], ones[:, 0:1],
                                            scr[:, d * TG:(d + 1) * TG],
                                            start=(d == 0), stop=(d == NCH - 1)),
                   waits=w, inc="s_pe" if (fine or dc == NCH - 1) else None)
            if fine:
                s1ch.append(M.val("s_pe"))
        st["ln3_s1_ch"] = s1ch
        st["pst_wr"] = M.val("s_pe")

    def emit_LN3_sqr(og, on_act=False, fine=False):
        if og < 0:
            return
        if fine:
            hs = []
            for hh in range(2):
                M.emit("scalar",
                       lambda e, h2=hh: e.activation(
                           scr[:, h2 * 4 * TG:(h2 + 1) * 4 * TG],
                           scr[:, h2 * 4 * TG:(h2 + 1) * 4 * TG],
                           AF.Square, scale=1.0),
                       waits=[("s_pe", st["ln3_s1_ch"][4 * hh + 3])], inc="s_act")
                hs.append(M.val("s_act"))
            st["ln3_sq_h"] = hs
            st["ln3_sq"] = ("s_act", hs[-1])
            return
        w = [("s_pe", M.val("s_pe"))]
        if on_act:
            M.emit("scalar",
                   lambda e: e.activation(scr[:, :], scr[:, :], AF.Square,
                                          scale=1.0), waits=w, inc="s_act")
            st["ln3_sq"] = ("s_act", M.val("s_act"))
            return
        M.emit("gpsimd",
               lambda e: e.tensor_tensor(scr[:, :], scr[:, :], scr[:, :], op=OP.mult),
               waits=w, drain=True, inc="s_gp")
        st["ln3_sq"] = ("s_gp", M.val("s_gp"))

    def emit_LN3_mid(og, fine=False):
        if og < 0:
            return
        if not fine:
            st["ln3_s2"] = stats2_mms(lambda d: scr[:, d * TG:(d + 1) * TG],
                                      [st["ln3_sq"]], pstA)
            return
        for dc in range(NCH):
            w = list(st["pst_rd"].items()) if dc == 0 else []
            if dc % 4 == 0:
                w = w + [("s_act", st["ln3_sq_h"][dc // 4])]
            M.emit("tensor",
                   lambda e, d=dc: e.matmul(pstA[0:1, TG:2 * TG], ones[:, 0:1],
                                            scr[:, d * TG:(d + 1) * TG],
                                            start=False, stop=(d == NCH - 1),
                                            skip_group_check=True),
                   waits=w, inc="s_pe" if dc == NCH - 1 else None)
        st["pst_wr"] = M.val("s_pe")
        st["ln3_s2"] = M.val("s_pe")

    def emit_LN3_B(og):
        if og < 0:
            return
        bc = bc_mms(st["ln3_rows"][og], rowsA)
        st["rowsA_war"] = bc
        oq = {}
        for dc in range(NCH):
            w2 = [("s_pe", bc)] if dc == 0 else ()
            M.emit("vector",
                   lambda e, d=dc: e.tensor_tensor(xs[:, d * TG:(d + 1) * TG],
                                                   xs[:, d * TG:(d + 1) * TG],
                                                   pbc[:, 0:TG], op=OP.subtract),
                   waits=w2, drain=(dc == 0))
            M.emit("vector",
                   lambda e, d=dc: e.tensor_tensor(xs[:, d * TG:(d + 1) * TG],
                                                   xs[:, d * TG:(d + 1) * TG],
                                                   pbc[:, TG:2 * TG], op=OP.mult),
                   drain=True,
                   inc="s_dve" if dc % 2 == 1 else None)
            if dc % 2 == 1:
                oq[dc // 2] = M.val("s_dve")
        st["out_q"] = oq
        st["out_done"][og] = M.val("s_dve")
        st["pbc_free"] = [("s_dve", st["out_done"][og])]
        if og == NG - 1:
            for q in range(4):
                M.emit("sync" if q % 2 == 0 else "scalar",
                       lambda e, gg=og, qq=q: e.dma_start(
                           out.ap()[qq * 256:(qq + 1) * 256,
                                    gg * TG:(gg + 1) * TG].rearrange(
                               "(c p) t -> p c t", p=128),
                           xs[:, qq * 2 * TG:(qq + 1) * 2 * TG].rearrange(
                               "p (c t) -> p c t", c=2)),
                       waits=[("s_dve", st["out_q"][q])], inc="s_out", amt=16)
        else:
            M.emit("sync",
                   lambda e, gg=og: e.dma_start(
                       out.ap()[:, gg * TG:(gg + 1) * TG].rearrange("(c p) t -> p c t", p=128),
                       xs[:, :].rearrange("p (c t) -> p c t", c=NCH)),
                   waits=[("s_dve", st["out_done"][og])], inc="s_out", amt=16)
        if og + 3 < NG:
            x_load(og + 3, og + 1)

    def emit_ln1_stats(lg):
        return stats_mms(lambda d: slot_sl(lg, d), [("s_dve", st["y1_evac"][lg])],
                         pstB, "pstB_free")

    def emit_sq(lg, on_act=False):
        w = [("s_dve", st["y1_evac"][lg]), ("s_pe", M.val("s_pe"))]
        if on_act:
            M.emit("scalar",
                   lambda e, gg=lg: e.activation(scr[:, :], slot(gg), AF.Square,
                                                 scale=1.0), waits=w, inc="s_act")
            return ("s_act", M.val("s_act"))
        M.emit("gpsimd",
               lambda e, gg=lg: e.tensor_tensor(scr[:, :], slot(gg), slot(gg),
                                               op=OP.mult),
               waits=w, drain=True, inc="s_gp")
        return ("s_gp", M.val("s_gp"))

    def emit_sq2(lg, on_act=False):
        w = [("s_dve", st["x1_done"][lg]), ("s_pe", M.val("s_pe"))]
        if on_act:
            M.emit("scalar",
                   lambda e, gg=lg: e.activation(scr[:, :], slot(gg), AF.Square,
                                                 scale=1.0), waits=w, inc="s_act")
            return ("s_act", M.val("s_act"))
        M.emit("gpsimd",
               lambda e, gg=lg: e.tensor_tensor(scr[:, :], slot(gg), slot(gg),
                                               op=OP.mult),
               waits=w, drain=True, inc="s_gp")
        return ("s_gp", M.val("s_gp"))

    def emit_norm_stream(lg, bc):
        for dc in range(NCH):
            w = [("s_pe", bc)] if dc == 0 else ()
            M.emit("vector",
                   lambda e, d=dc, gg=lg: e.tensor_tensor(
                       slot_sl(gg, d), slot_sl(gg, d), pbc[:, 0:TG], op=OP.subtract),
                   waits=w, drain=(dc == 0))
        for dc in range(NCH):
            M.emit("vector",
                   lambda e, d=dc, gg=lg: e.tensor_tensor(
                       slot_sl(gg, d), slot_sl(gg, d), pbc[:, TG:2 * TG], op=OP.mult),
                   drain=(dc == 0 or dc == NCH - 1),
                   inc="s_dve" if dc == NCH - 1 else None)
        return M.val("s_dve")

    def emit_h(lg, fc1_wait, bc2, dve_gb=False):
        if dve_gb:
            chunks = []
            for dc in range(NCH):
                w = [("s_pe", max(fc1_wait, bc2))] if dc == 0 else ()
                M.emit("vector",
                       lambda e, d=dc, gg=lg: e.tensor_tensor(
                           h_b[:, d * TG:(d + 1) * TG], slot_sl(gg, d),
                           pbc[:, TG:2 * TG], op=OP.mult),
                       waits=w, drain=(dc == 0))
                M.emit("vector",
                       lambda e, d=dc: e.tensor_scalar(
                           h_b[:, d * TG:(d + 1) * TG], h_b[:, d * TG:(d + 1) * TG],
                           gam[:, d:d + 1], bet[:, d:d + 1], op0=OP.mult, op1=OP.add),
                       drain=True, inc="s_dve")
                chunks.append(M.val("s_dve"))
            st["h_chunks"][lg] = chunks
            st["h_mul_done"] = chunks[-1]
            st["h_done"][lg] = ("s_dve", M.val("s_dve"))
            return
        mul_vals = []
        for dc in range(NCH):
            w = [("s_pe", max(fc1_wait, bc2))] if dc == 0 else ()
            M.emit("vector",
                   lambda e, d=dc, gg=lg: e.tensor_tensor(
                       h_b[:, d * TG:(d + 1) * TG], slot_sl(gg, d),
                       pbc[:, TG:2 * TG], op=OP.mult),
                   waits=w, drain=(dc == 0), inc="s_dve")
            mul_vals.append(M.val("s_dve"))
        st["h_mul_done"] = mul_vals[-1]
        for dc in range(NCH):
            M.emit("gpsimd",
                   lambda e, d=dc: e.tensor_scalar(
                       h_b[:, d * TG:(d + 1) * TG], h_b[:, d * TG:(d + 1) * TG],
                       gam[:, d:d + 1], bet[:, d:d + 1], op0=OP.mult, op1=OP.add),
                   waits=[("s_dve", mul_vals[dc])],
                   drain=(dc == NCH - 1), inc="s_gp" if dc == NCH - 1 else None)
        st["h_done"][lg] = ("s_gp", M.val("s_gp"))

    def warmup_hooks(lg):
        wst = {}
        def h0():
            wst["s1"] = emit_ln1_stats(lg)
        def h1():
            wst["sq"] = emit_sq(lg, on_act=True)
        def h2():
            wst["s2"] = stats2_mms(lambda d: scr[:, d * TG:(d + 1) * TG],
                                   [wst["sq"]], pstB)
            wst["fr1"] = row_front_B(rowsB, wst["s2"], st["rowsB_war"])
        def h3():
            wst["rows1"] = row_tail(rowsB, varB, widen, wst["fr1"])
        def h4():
            bc1 = bc_mms(wst["rows1"], rowsB)
            wst["bc1"] = bc1
            x1d = emit_norm_stream(lg, bc1)
            st["x1_done"][lg] = x1d
            st["pbc_free"] = [("s_dve", x1d)]
        def h8():
            sqb = emit_sq2(lg, on_act=True)
            wst["s2b"] = stats2a_mms(lambda d: scr[:, d * TG:(d + 1) * TG],
                                     [sqb], pstB)
            wst["fr2"] = row_front_V(rowsB, wst["s2b"])
        return wst, {0: h0, 1: h1, 2: h2, 3: h3, 4: h4, 8: h8}

    def warmup_tail(lg, wst):
        rows2 = row_tail(rowsB, varB, widen, wst["fr2"])
        bc2 = bc2V_mms(rows2, rowsB)
        st["rowsB_war"] = bc2
        emit_h(lg, st["fc1_done"].get(lg - 1, 0), bc2, dve_gb=True)
        st["pbc_free"] = [("s_dve", st["h_mul_done"])]
        M.emit("scalar", lambda e: e.activation(dmy[0:1, 0:1], dmy[0:1, 0:1],
                                                AF.Silu, scale=1.0))

    def emit_zevac(fg, fc2_stop):
        xs_w = [("s_pe", fc2_stop), ("s_dve", st["x1_done"][fg])]
        if fg + 2 < NG:
            xs_w.append(xb_done[fg + 2])
        else:
            xs_w.append(("s_out", 16 * fg))
        z_vals = []
        for oc in range(NCH):
            w = xs_w if oc == 0 else ()
            M.emit("vector",
                   lambda e, o=oc, gg=fg: e.scalar_tensor_tensor(
                       xs[:, o * TG:(o + 1) * TG], pzacc[:, o * TG:(o + 1) * TG], 0.5,
                       slot_sl(gg, o), op0=OP.mult, op1=OP.add),
                   waits=w, drain=(oc == 0 or oc == NCH - 1), inc="s_dve")
            z_vals.append(M.val("s_dve"))
        st["zstt_done"][fg] = z_vals[-1]
        y2c = []
        for oc in range(NCH):
            M.emit("gpsimd",
                   lambda e, o=oc: e.tensor_scalar(
                       xs[:, o * TG:(o + 1) * TG], xs[:, o * TG:(o + 1) * TG],
                       onesc[:, 0:1], b2h[:, o:o + 1], op0=OP.mult, op1=OP.add),
                   waits=[("s_dve", z_vals[oc])],
                   drain=(oc == NCH - 1), inc="s_gp")
            y2c.append(M.val("s_gp"))
        st["y2_chunks"][fg] = y2c
        st["y2_done"][fg] = M.val("s_gp")

    def consume_defer_h():
        d = st.pop("defer_h", None)
        if d is None:
            return
        lg_, fc1a, r2 = d
        bc2 = bc2V_mms(r2, rowsB)
        st["rowsB_war"] = bc2
        emit_h(lg_, fc1a, bc2, dve_gb=True)
        st["pbc_free"] = [("s_dve", st["h_mul_done"])]

    def emit_F(fg, ln3_og, copy_og=None):
        lg = fg + 1
        do_ln = lg < NG
        scratch = lg + 1
        silu_done = {}
        ctx = {}

        def emit_fc1(fc):
            bank = fc % 2
            pwork = pa if bank == 0 else pb
            w = [st["h_done"][fg]]
            if fc >= 2:
                w.append(("s_act", silu_done[fc - 2]))
            hch = st["h_chunks"].get(fg) if fc == 0 else None
            for hc in range(NCH):
                if hch is not None:
                    wc = [("s_dve", hch[hc])]
                elif hc == 0:
                    wc = w
                else:
                    wc = ()
                M.emit("tensor",
                       lambda e, f=fc, h=hc, p=pwork: e.matmul(
                           p[:, 0:TG],
                           w1b[:, h * F + f * 128: h * F + (f + 1) * 128],
                           h_b[:, h * TG:(h + 1) * TG],
                           start=(h == 0), stop=(h == NCH - 1)),
                       waits=wc, inc="s_pe" if hc == NCH - 1 else None)
            mm_done = M.val("s_pe")
            M.emit("scalar",
                   lambda e, f=fc, p=pwork: e.activation(
                       fey[:, (f % 4) * TG:(f % 4 + 1) * TG], p[:, 0:TG], AF.Silu,
                       bias=b1c[:, f:f + 1], scale=1.0),
                   waits=[("s_pe", mm_done)], inc="s_act")
            silu_done[fc] = M.val("s_act")

        def emit_fc2(FB):
            for j in range(4):
                fc = FB * 4 + j
                sd = silu_done[fc]
                for oc in range(NCH):
                    first = (FB == 0 and j == 0 and oc % 2 == 0)
                    last = (FB == 7 and j == 3 and oc % 2 == 1)
                    w = [("s_act", sd)]
                    if first and oc == 0:
                        w.append(("s_dve", max(st["y1_evac"][min(fg + 1, NG - 1)],
                                               st["zstt_done"].get(fg - 1, 0))))
                    M.emit("tensor",
                           lambda e, f=fc, o=oc, jj=j, fi=first, la=last: e.matmul(
                               pzacc[:, o * TG:(o + 1) * TG],
                               w2b[:, f * D + o * 128: f * D + (o + 1) * 128],
                               fey[:, jj * TG:(jj + 1) * TG],
                               start=fi, stop=la,
                               skip_group_check=not (fi or la)),
                           waits=w if oc == 0 else (),
                           inc="s_pe" if (FB == 7 and oc == NCH - 1 and j == 3) else None)

        for FB in range(8):
            for j in range(4):
                emit_fc1(FB * 4 + j)
            if FB == 7:
                ctx["fc1_tail"] = M.val("s_pe")
            emit_fc2(FB)
            if FB == 0:
                if copy_og is not None:
                    emit_LN3_copy(copy_og)
                emit_LN3_s1(ln3_og)
                if do_ln:
                    ctx["s1"] = emit_ln1_stats(lg)
            if FB == 1:
                emit_LN3_sqr(ln3_og, on_act=True)
                emit_LN3_mid(ln3_og)
                if ln3_og >= 0:
                    ctx["frA"] = row_front_A(rowsA, st["ln3_s2"], st["rowsA_war"])
            if FB == 2:
                if do_ln:
                    ctx["sq"] = emit_sq(lg, on_act=True)
                    ctx["s2_1"] = stats2_mms(lambda d: scr[:, d * TG:(d + 1) * TG],
                                             [ctx["sq"]], pstB)
                if ln3_og >= 0:
                    st["ln3_rows"][ln3_og] = row_tail(rowsA, varA, widen, ctx["frA"])
                    M.emit("scalar", lambda e: e.activation(
                        dmy[0:1, 0:1], dmy[0:1, 0:1], AF.Silu, scale=1.0))
                if do_ln:
                    ctx["frB"] = row_front_B(rowsB, ctx["s2_1"], st["rowsB_war"])
            if FB == 3:
                if do_ln:
                    M.emit("scalar", lambda e: e.activation(
                        dmy[0:1, 0:1], onesc[0:1, 0:1], AF.Ln, scale=1.0))
                    ctx["rows1"] = row_tail(rowsB, varB, widen, ctx["frB"])
                    M.emit("scalar", lambda e: e.activation(
                        dmy[0:1, 0:1], dmy[0:1, 0:1], AF.Silu, scale=1.0))
                emit_LN3_B(ln3_og)
            if FB == 4 and do_ln:
                ctx["bc1"] = bc_mms(ctx["rows1"], rowsB)
                x1d = emit_norm_stream(lg, ctx["bc1"])
                st["x1_done"][lg] = x1d
                st["pbc_free"] = [("s_dve", x1d)]
            if FB == 5:
                if do_ln:
                    if lg == NG - 1:
                        ctx["sqb"] = emit_sq2(lg, on_act=True)
                        ctx["s2b"] = stats2a_mms(
                            lambda d: scr[:, d * TG:(d + 1) * TG],
                            [ctx["sqb"]], pstB)
                        ctx["fr2"] = row_front_V(rowsB, ctx["s2b"])
                        M.emit("scalar", lambda e: e.activation(
                            dmy[0:1, 0:1], onesc[0:1, 0:1], AF.Ln, scale=1.0))
                        ctx["rows2"] = row_tail(rowsB, varB, widen, ctx["fr2"])
                        M.emit("scalar", lambda e: e.activation(
                            dmy[0:1, 0:1], dmy[0:1, 0:1], AF.Silu, scale=1.0))
                    else:
                        ctx["sqb"] = emit_sq2(lg)
                if 3 <= fg + 2 < NG:
                    xb_convert(fg + 2,
                               extra_waits=[("s_gp", st["y2_done"].get(fg - 1, 0))])
            if FB == 6:
                if do_ln:
                    if lg == NG - 1:
                        ctx["bc2f"] = bc2V_mms(ctx["rows2"], rowsB)
                    else:
                        ctx["s2b"] = stats2a_mms(lambda d: scr[:, d * TG:(d + 1) * TG],
                                                 [ctx["sqb"]], pstB)
                        ctx["fr2"] = row_front_V(rowsB, ctx["s2b"])
                if fg + 2 < NG:
                    emit_colsq_sq(fg + 2, M.val("s_pe"))
            if FB == 7:
                fc2_stop = M.val("s_pe")
                if do_ln and lg == NG - 1:
                    st["rowsB_war"] = ctx["bc2f"]
                    emit_h(lg, ctx["fc1_tail"], ctx["bc2f"], dve_gb=True)
                    st["pbc_free"] = [("s_dve", st["h_mul_done"])]
                emit_zevac(fg, fc2_stop)
                if do_ln:
                    if lg == NG - 1:
                        pass
                    else:
                        ctx["rows2"] = row_tail(rowsB, varB, widen, ctx["fr2"])
                        st["defer_h"] = (lg, ctx["fc1_tail"], ctx["rows2"])
                if FB == 7 and fg + 2 < NG:
                    emit_colsq_stats(fg + 2)
        st["fc1_done"][fg] = M.val("s_pe")
        st["fc2_done"][fg] = M.val("s_pe")
        st["silu_last"] = M.val("s_act")
        st["silu_pa"] = silu_done[30]
        st["silu_pb"] = silu_done[31]

    emit_colsq_sq(0, M.val("s_pe"), on_act=True)
    emit_colsq_stats(0)
    M.emit("scalar", lambda e: e.dma_start(
        xs[:, :].rearrange("p (c t) -> p c t", c=NCH),
        xk.ap()[:, TG:2 * TG].rearrange("(c p) t -> p c t", p=128)),
        waits=[("s_act", xb0_done)], inc="s_x0", amt=16)
    M.emit("vector", lambda e: e.tensor_copy(pool[:, 2048:4096], xs[:, :]),
           waits=[("s_x0", M.val("s_x0")), ("s_dve", M.val("s_dve"))],
           inc="s_dve")
    xb_done[1] = ("s_dve", M.val("s_dve"))

    for g in range(NG):
        if g == 0:
            emit_X(g, wu={5: lambda: emit_colsq_sq(1, st["colsq_done"][0],
                                                   on_act=True),
                          7: lambda: emit_colsq_stats(1)})
            M.emit("scalar", lambda e: e.dma_start(
                xs[:, :].rearrange("p (c t) -> p c t", c=NCH),
                xk.ap()[:, 2 * TG:3 * TG].rearrange("(c p) t -> p c t", p=128)),
                waits=[xb_done[1]], inc="s_x0", amt=16)
            M.emit("vector", lambda e: e.tensor_copy(pool[:, 4096:6144], xs[:, :]),
                   waits=[("s_x0", M.val("s_x0")), ("s_dve", M.val("s_dve"))],
                   inc="s_dve")
            xb_done[2] = ("s_dve", M.val("s_dve"))
        elif g == 1:
            wst, wuh = warmup_hooks(0)
            emit_X(g, wu=wuh)
            warmup_tail(0, wst)
        else:
            emit_X(g)
        emit_LN3_copy(g - 2, war=st["colsq_done"].get(g))
        if g >= 1:
            emit_F(g - 1, g - 2)
    consume_defer_h()
    emit_F(NG - 1, NG - 2, copy_og=NG - 2)
    emit_LN3_copy(NG - 1, fine=True)
    emit_LN3_s1(NG - 1, fine=True)
    emit_LN3_sqr(NG - 1, fine=True)
    M.emit("scalar", lambda e: e.activation(dmy[0:1, 0:1], onesc[0:1, 0:1],
                                            AF.Ln, scale=1.0))
    emit_LN3_mid(NG - 1, fine=True)
    frA_t = row_front_A(rowsA, st["ln3_s2"], st["rowsA_war"])
    st["ln3_rows"][NG - 1] = row_tail(rowsA, varA, widen, frA_t)
    emit_LN3_B(NG - 1)

    M.emit("scalar", None, waits=[("s_out", M.val("s_out"))])

    blockM = es.enter_context(nc.Block("main"))
    M.replay(blockM, handles)
    es.close()
    return nc


def _selector(k):
    import ml_dtypes
    s = np.zeros((k, k * 128), dtype=ml_dtypes.bfloat16)
    for c in range(k):
        s[c, c * 128:(c + 1) * 128] = 1
    return s


_NC_CACHE = {}


def _get_nc():
    if "nc" not in _NC_CACHE:
        _NC_CACHE["nc"] = build()
    return _NC_CACHE["nc"]


def kernel(L0_attn_x, L0_attn_input_weights, L0_attn_xis, L0_attn_proj_matrix,
           L0_ffn_ln_gamma, L0_ffn_ln_beta, L0_ffn_fc1_q, L0_ffn_fc1_scales,
           L0_ffn_fc1_bias, L0_ffn_fc2_q, L0_ffn_fc2_scales, L0_ffn_fc2_bias):
    c = np.ascontiguousarray
    x_fm = c(np.asarray(L0_attn_x, np.float32).reshape(-1, D).T)
    gTh = c(np.asarray(L0_attn_proj_matrix, np.float32).T)
    iwTh = c(np.asarray(L0_attn_input_weights, np.float32).T)
    xis = np.asarray(L0_attn_xis, np.float32)
    shared = dict(
        gT=gTh, iwT=iwTh,
        xisr=c(xis.reshape(1, R)),
        xispp=c(xis.reshape(NCH, 128).T),
        gampp=c(np.asarray(L0_ffn_ln_gamma, np.float32).reshape(NCH, 128).T),
        betpp=c(np.asarray(L0_ffn_ln_beta, np.float32).reshape(NCH, 128).T),
        b2pp=c(np.asarray(L0_ffn_fc2_bias, np.float32).reshape(NCH, 128).T),
        b1pp=c(np.asarray(L0_ffn_fc1_bias, np.float32).reshape(FCH, 128).T),
        q1=c(np.asarray(L0_ffn_fc1_q).astype(np.uint8)),
        q2=c(np.asarray(L0_ffn_fc2_q).astype(np.uint8)),
        s1c=c(np.asarray(L0_ffn_fc1_scales, np.float32)),
        s2c=c(np.asarray(L0_ffn_fc2_scales, np.float32)),
        sel8=_selector(8),
        sel32=_selector(32),
    )
    in_maps = []
    for k in range(NCORES):
        m = dict(shared)
        m["xk"] = c(x_fm[:, k * TCORE:(k + 1) * TCORE])
        in_maps.append(m)

    nc = _get_nc()
    res = run_bass_kernel_spmd(nc, in_maps, core_ids=list(range(NCORES)))
    out_fm = np.concatenate([res.results[k]["out"] for k in range(NCORES)], axis=1)
    return c(out_fm.T).reshape(2, 8192, D)


# version 67
# speedup vs baseline: 1.0071x; 1.0071x over previous
import contextlib
import numpy as np
import concourse.bass as bass
import concourse.mybir as mybir
from concourse.bass_utils import run_bass_kernel_spmd
from concourse.alu_op_type import AluOpType

F32 = mybir.dt.float32
BF16 = mybir.dt.bfloat16
I32 = mybir.dt.int32
U8 = mybir.dt.uint8
AF = mybir.ActivationFunctionType
OP = AluOpType

D = 1024
NCH = 8
R = 1024
F = 4096
FCH = 32
TCORE = 2048
TG = 256
NG = TCORE // TG
NCORES = 8
LN1024 = float(np.log(1024.0))

ENGINES = ("sync", "tensor", "vector", "scalar", "gpsimd")


class Prog:
    def __init__(self):
        self.ops = {e: [] for e in ENGINES}
        self.cnt = {}
        self.lastwait = {}

    def val(self, sem):
        return self.cnt.get(sem, 0)

    def emit(self, eng, fn, waits=(), inc=None, amt=1, drain=False):
        merged = {}
        for (s, v) in waits:
            if v > 0 and merged.get(s, -1) < v:
                merged[s] = v
        ws = list(merged.items())
        if inc is not None:
            self.cnt[inc] = self.cnt.get(inc, 0) + amt
        self.ops[eng].append((ws, fn, inc, amt, drain))
        return self.cnt.get(inc, 0) if inc is not None else None

    def replay(self, block, handles):
        for eng_name in ENGINES:
            if not self.ops[eng_name]:
                continue

            def body(eng, _ops=self.ops[eng_name]):
                for ws, fn, inc, amt, drain in _ops:
                    if drain:
                        eng.drain()
                    for (s, v) in ws:
                        eng.wait_ge(handles[s], v)
                    inst = fn(eng) if fn is not None else None
                    if inc is not None:
                        inst.then_inc(handles[inc], amt)

            getattr(block, eng_name)(body)


def build():
    nc = bass.Bass("TRN2", target_bir_lowering=False)

    xk = nc.dram_tensor("xk", [D, TCORE], F32, kind="ExternalInput")
    gT = nc.dram_tensor("gT", [D, R], F32, kind="ExternalInput")
    iwT = nc.dram_tensor("iwT", [D, D], F32, kind="ExternalInput")
    xisr = nc.dram_tensor("xisr", [1, R], F32, kind="ExternalInput")
    xispp = nc.dram_tensor("xispp", [128, NCH], F32, kind="ExternalInput")
    gampp = nc.dram_tensor("gampp", [128, NCH], F32, kind="ExternalInput")
    betpp = nc.dram_tensor("betpp", [128, NCH], F32, kind="ExternalInput")
    b2pp = nc.dram_tensor("b2pp", [128, NCH], F32, kind="ExternalInput")
    b1pp = nc.dram_tensor("b1pp", [128, FCH], F32, kind="ExternalInput")
    q1 = nc.dram_tensor("q1", [D, F], U8, kind="ExternalInput")
    q2 = nc.dram_tensor("q2", [F, D], U8, kind="ExternalInput")
    s1c = nc.dram_tensor("s1c", [8, F], F32, kind="ExternalInput")
    s2c = nc.dram_tensor("s2c", [32, D], F32, kind="ExternalInput")
    sel8 = nc.dram_tensor("sel8", [8, 1024], BF16, kind="ExternalInput")
    sel32 = nc.dram_tensor("sel32", [32, F], BF16, kind="ExternalInput")
    out = nc.dram_tensor("out", [D, TCORE], F32, kind="ExternalOutput")

    es = contextlib.ExitStack()
    sb = lambda name, shape, dt: es.enter_context(nc.sbuf_tensor(name, shape, dt))
    ps = lambda name, shape: es.enter_context(nc.psum_tensor(name, shape, F32))

    w1b = sb("w1b", [128, NCH * F], BF16)
    w2b_off = nc.sbuf_base
    w2b = sb("w2b", [128, FCH * D], BF16)
    gTb = sb("gTb", [128, NCH * R], BF16)
    WT = sb("WT", [128, NCH * D], BF16)
    NQS = 8
    xis_pp = sb("xis_pp", [128, NCH], F32)
    bias_f = sb("bias_f", [128, NCH], F32)
    bias_w = sb("bias_w", [128, NCH], F32)
    gam = sb("gam", [128, NCH], F32)
    bet = sb("bet", [128, NCH], F32)
    b2h = sb("b2h", [128, NCH], F32)
    b1c = sb("b1c", [128, FCH], F32)
    ones = sb("ones", [128, 128], BF16)
    eps_t = sb("eps_t", [1, 1], F32)
    xs_off = (nc.sbuf_base + 31) // 32 * 32
    xs = sb("xs", [128, NCH * TG], F32)
    pool_off = (nc.sbuf_base + 31) // 32 * 32
    pool = sb("pool", [128, 3 * NCH * TG], BF16)
    hb_off = nc.sbuf_base
    h_b = sb("h_b", [128, NCH * TG], BF16)
    fey = sb("fey", [128, 4 * TG], BF16)
    scr_off = nc.sbuf_base
    scr = sb("scr", [128, NCH * TG], BF16)
    NQS_ = 10
    qst = nc.alloc_sbuf_tensor_at("qst", [128, 6 * 1024], U8, offset=hb_off)
    qst3 = nc.alloc_sbuf_tensor_at("qst3", [128, 4 * 1024], U8, offset=pool_off + 8192)

    def qslot_ap(j):
        if j < 6:
            return qst[:, j * 1024:(j + 1) * 1024]
        return qst3[:, (j - 6) * 1024:(j - 5) * 1024]
    ab_row = sb("ab_row", [1, R], BF16)
    rwsb = sb("rwsb", [1, 2 * TG], F32)
    ry = sb("ry", [1, TG], F32)
    rowsA = {"nmu": sb("nmu_bA", [1, TG], BF16), "rh": sb("rb_hA", [1, TG], BF16),
             "rl": sb("rb_lA", [1, TG], BF16)}
    rowsB = {"nmu": sb("nmu_bB", [1, TG], BF16), "rh": sb("rb_hB", [1, TG], BF16),
             "rl": sb("rb_lB", [1, TG], BF16)}
    delta_b = sb("delta_b", [1, TG], BF16)
    onesc = sb("onesc", [128, 1], F32)
    dmy = sb("dmy", [1, 1], F32)

    sem_names = ["s_wp", "s_st", "s_st1", "s_x", "s_x0", "s_out", "s_dq",
                 "s_pe", "s_act", "s_dve", "s_gp"] + [f"s_q{j}" for j in range(12)]
    handles = {}
    for s in sem_names:
        handles[s] = es.enter_context(nc.semaphore(s))

    with contextlib.ExitStack() as pes:
        assert pool_off == xs_off + 8192
        iwTb = nc.alloc_sbuf_tensor_at("iwTb", [128, NCH * 1024], BF16, offset=xs_off)
        stagef = nc.alloc_sbuf_tensor_at("stagef", [128, 1024], F32, offset=w2b_off)
        stagef2 = nc.alloc_sbuf_tensor_at("stagef2", [128, 1024], F32, offset=w2b_off + 4096)
        sqs = nc.alloc_sbuf_tensor_at("sqs", [128, 1024], BF16, offset=w2b_off + 8192)
        sqs2 = nc.alloc_sbuf_tensor_at("sqs2", [128, 1024], BF16, offset=w2b_off + 10240)
        s1f = nc.alloc_sbuf_tensor_at("s1f", [8, F], F32, offset=w2b_off + 12288)
        s1b = nc.alloc_sbuf_tensor_at("s1b", [8, F], BF16, offset=w2b_off + 28672)
        s2f = nc.alloc_sbuf_tensor_at("s2f", [32, D], F32, offset=w2b_off + 36864)
        sel8b = nc.alloc_sbuf_tensor_at("sel8b", [8, 1024], BF16, offset=w2b_off + 40960)
        dw_row = nc.alloc_sbuf_tensor_at("dw_row", [1, D], BF16, offset=scr_off + 2048)
        sel32b = pes.enter_context(nc.sbuf_tensor("sel32b", [32, F], BF16))
        s2b = nc.alloc_sbuf_tensor_at("s2b", [32, D], BF16, offset=scr_off)
        pW = pes.enter_context(nc.psum_tensor("pW", [128, 1024], F32))
        ppst_t = pes.enter_context(nc.psum_tensor("ppst", [128, 512], F32))
        ppst2_t = pes.enter_context(nc.psum_tensor("ppst2", [128, 512], F32))
        ppst = ppst_t[0:1, :]
        ppst2 = ppst2_t[0:1, :]
        pw_slots = [pW[:, 0:512], pW[:, 512:1024], ppst_t[:, :], ppst2_t[:, :]]
        psc = pes.enter_context(nc.psum_tensor("psc", [128, 2048], F32))

        P = Prog()
        P.emit("sync", lambda e: e.dma_start(stagef[0:1, 0:R], xisr.ap()),
               inc="s_x", amt=16)
        for (dst, src) in [(xis_pp, xispp), (gam, gampp),
                           (bet, betpp), (b2h, b2pp), (b1c, b1pp),
                           (s1f, s1c), (s2f, s2c), (sel8b, sel8),
                           (sel32b, sel32)]:
            P.emit("sync", lambda e, d=dst, s=src: e.dma_start(d[:, :], s.ap()),
                   inc="s_wp", amt=16)
        vecs_loaded = P.val("s_wp")

        P.emit("vector", lambda e: e.memset(ones[:, :], 1.0))
        P.emit("vector", lambda e: e.memset(eps_t[:, :], 1e-5))
        P.emit("vector", lambda e: e.memset(onesc[:, :], 1.0))
        P.emit("vector", lambda e: e.memset(dmy[:, :], 1.0))
        P.emit("vector", lambda e: e.tensor_copy(ab_row[:, :], stagef[0:1, 0:R]),
               waits=[("s_x", 16)], inc="s_dve")
        ab_done = P.val("s_dve")

        stage_bufs = [stagef, stagef2]
        stage_conv = {0: 0, 1: 0}

        def stage_eighth(src, qi, dst_b, k):
            b = k % 2
            sem = "s_st" if b == 0 else "s_st1"
            w = [("s_act", stage_conv[b])] + ([("s_dve", ab_done)] if k == 0 else [])
            P.emit("sync" if b == 0 else "scalar",
                   lambda e, s=src, h=qi, bb=b: e.dma_start(
                       stage_bufs[bb][:, :],
                       s.ap()[h * 128:(h + 1) * 128, :]),
                   waits=w, inc=sem, amt=16)
            dmas = P.val(sem)
            P.emit("scalar",
                   lambda e, d=dst_b, h=qi, bb=b: e.activation(
                       d[:, h * 1024:(h + 1) * 1024], stage_bufs[bb][:, :],
                       AF.Copy, scale=1.0),
                   waits=[(sem, dmas)], inc="s_act")
            stage_conv[b] = P.val("s_act")
            return stage_conv[b]

        P.emit("vector", lambda e: e.tensor_copy(s1b[:, :], s1f[:, :]),
               waits=[("s_wp", vecs_loaded)], inc="s_dve")
        P.emit("vector", lambda e: e.tensor_copy(s2b[:, :], s2f[:, :]),
               inc="s_dve")
        scales_conv = P.val("s_dve")
        iw_conv = {}
        gt_conv = {}

        P.emit("vector", lambda e: e.tensor_tensor(bias_f[:, :], xis_pp[:, :], xis_pp[:, :], op=OP.mult),
               waits=[("s_wp", vecs_loaded)])
        P.emit("vector", lambda e: e.tensor_scalar(bias_f[:, :], bias_f[:, :], -512.0, None, op0=OP.mult),
               drain=True)
        P.emit("vector", lambda e: e.tensor_scalar(bias_w[:, :], xis_pp[:, :], 2.0, -LN1024, op0=OP.mult, op1=OP.add),
               waits=[("s_wp", vecs_loaded)])
        P.emit("vector", lambda e: e.tensor_scalar(b2h[:, :], b2h[:, :], 0.5, None, op0=OP.mult),
               waits=[("s_wp", vecs_loaded)], drain=True, inc="s_dve")
        vprep = P.val("s_dve")

        def dq_macro(m):
            if m < 32:
                fq2, hc = divmod(m, 8)
                qsrc = q1.ap()[hc * 128:(hc + 1) * 128, fq2 * 1024:(fq2 + 1) * 1024]
                lhsT = sel8b[0:8, hc * 128:(hc + 1) * 128]
                rhs0 = s1b[0:8, fq2 * 1024: fq2 * 1024 + 512]
                rhs1 = s1b[0:8, fq2 * 1024 + 512: (fq2 + 1) * 1024]
                dst = w1b[:, hc * F + fq2 * 1024: hc * F + (fq2 + 1) * 1024]
            else:
                fc = m - 32
                qsrc = q2.ap()[fc * 128:(fc + 1) * 128, :]
                lhsT = sel32b[0:32, fc * 128:(fc + 1) * 128]
                rhs0 = s2b[0:32, 0:512]
                rhs1 = s2b[0:32, 512:1024]
                dst = w2b[:, fc * D:(fc + 1) * D]
            return qsrc, lhsT, rhs0, rhs1, dst

        last_user = {}

        def emit_dq_dma(m):
            qsl = m % NQS_
            qq = "gpsimd" if qsl < 4 else ("scalar" if qsl < 7 else "sync")
            qsrc, lhsT, rhs0, rhs1, dst = dq_macro(m)
            prev = last_user.get(qsl)
            last_user[qsl] = m
            w = [("s_dq", prev + 1)] if prev is not None else []
            if m >= 32:
                w += [("s_st", 128), ("s_st1", 128)]
            P.emit(qq,
                   lambda e, q=qsrc, sl=qsl: e.dma_start(qslot_ap(sl), q),
                   waits=w, inc=f"s_q{qsl}", amt=16)
            return qsl, P.val(f"s_q{qsl}"), lhsT, rhs0, rhs1, dst

        w2_gate = {"s_act": 0, "s_pe": 0}

        def emit_dq_compute(m, issued):
            qsl, qv, lhsT, rhs0, rhs1, dst = issued
            pair = m % 2
            P.emit("tensor",
                   lambda e, l=lhsT, r=rhs0, p=pair: e.matmul(
                       psc[:, p * 1024: p * 1024 + 512], l, r, start=True, stop=True),
                   waits=[("s_dve", scales_conv), ("s_dq", m - 1)])
            P.emit("tensor",
                   lambda e, l=lhsT, r=rhs1, p=pair: e.matmul(
                       psc[:, p * 1024 + 512:(p + 1) * 1024], l, r,
                       start=True, stop=True), inc="s_pe")
            bc_done = P.val("s_pe")
            w_stt = [(f"s_q{qsl}", qv), ("s_pe", bc_done)]
            if m >= 32:
                w_stt += [("s_act", w2_gate["s_act"]), ("s_pe", w2_gate["s_pe"])]
            P.emit("vector",
                   lambda e, d=dst, sl=qsl, p=pair: e.scalar_tensor_tensor(
                       d, qslot_ap(sl), -128.0,
                       psc[:, p * 1024:(p + 1) * 1024], op0=OP.add, op1=OP.mult),
                   waits=w_stt, inc="s_dq")

        dq_issued = {}
        dq_next = [0]

        def needs_lookahead(m):
            return m >= NQS_

        def emit_dq(n):
            for _ in range(n):
                m = dq_next[0]
                if m >= 64:
                    return
                dq_next[0] = m + 1
                if m not in dq_issued:
                    dq_issued[m] = emit_dq_dma(m)
                m2 = m + 7
                if (m2 < 64 and needs_lookahead(m2) and m2 not in dq_issued
                        and (m2 < 32 or m >= 32)):
                    dq_issued[m2] = emit_dq_dma(m2)
                emit_dq_compute(m, dq_issued[m])

        for m in range(NQS_):
            dq_issued[m] = emit_dq_dma(m)


        for dc in range(NCH):
            iw_conv[dc] = stage_eighth(iwT, dc, iwTb, k=dc)
            for oh in range(2):
                sl = (dc * 2 + oh) % 4
                w = ([("s_pe", P.val("s_pe"))] if dc >= 2 else []) + \
                    [("s_act", iw_conv[dc])]
                P.emit("gpsimd",
                       lambda e, d=dc, o=oh, s=sl: e.tensor_tensor(
                           sqs[:, s * 512:(s + 1) * 512] if s < 2 else
                           sqs2[:, (s - 2) * 512:(s - 1) * 512],
                           iwTb[:, d * 1024 + o * 512: d * 1024 + (o + 1) * 512],
                           iwTb[:, d * 1024 + o * 512: d * 1024 + (o + 1) * 512],
                           op=OP.mult),
                       waits=w, drain=True, inc="s_gp")
                sq_done = P.val("s_gp")
                P.emit("tensor",
                       lambda e, d=dc, o=oh, s=sl: e.matmul(
                           (ppst if o == 0 else ppst2)[0:1, 0:512], ones[:, 0:1],
                           sqs[:, s * 512:(s + 1) * 512] if s < 2 else
                           sqs2[:, (s - 2) * 512:(s - 1) * 512],
                           start=(d == 0), stop=(d == NCH - 1),
                           skip_group_check=(0 < d < NCH - 1)),
                       waits=[("s_gp", sq_done), ("s_dve", vprep)], inc="s_pe")
            if dc == NCH - 1:
                sqn_done = P.val("s_pe")
            emit_dq(4)
        for qi in range(8):
            gt_conv[qi] = stage_eighth(gT, qi, gTb, k=8 + qi)
        w2_gate["s_act"] = max(stage_conv[0], stage_conv[1])
        w2_gate["s_pe"] = sqn_done
        for oh in range(2):
            P.emit("scalar",
                   lambda e, o=oh: e.activation(
                       dw_row[0:1, o * 512:(o + 1) * 512],
                       (ppst if o == 0 else ppst2)[0:1, 0:512],
                       AF.Copy, scale=-0.5),
                   waits=[("s_pe", sqn_done)], inc="s_act")
        dw_done = P.val("s_act")

        evac_vals = {}

        def wseries_tail(idx, mm_done_inc=True):
            rc, oh = divmod(idx, 2)
            slot_i = idx % 4
            P.emit("tensor",
                   lambda e, r=rc, o=oh, s=slot_i: e.matmul(
                       pw_slots[s],
                       ab_row[0:1, r * 128:(r + 1) * 128],
                       dw_row[0:1, o * 512:(o + 1) * 512],
                       start=False, stop=False, skip_group_check=True),
                   inc="s_pe")
            mm_done = P.val("s_pe")
            P.emit("scalar",
                   lambda e, r=rc, o=oh, s=slot_i: e.activation(
                       WT[:, r * D + o * 512: r * D + (o + 1) * 512],
                       pw_slots[s], AF.Exp,
                       bias=bias_w[:, r:r + 1], scale=xis_pp[:, r:r + 1]),
                   waits=[("s_pe", mm_done)], inc="s_act")
            evac_vals[idx] = P.val("s_act")
            return mm_done

        for dc in range(NCH):
            for idx in range(4):
                rc, oh = divmod(idx, 2)
                w_dc = [("s_act", gt_conv[dc])]
                if dc == 0:
                    w_dc.append(("s_act", dw_done))
                P.emit("tensor",
                       lambda e, r=rc, o=oh, d=dc, s=idx: e.matmul(
                           pw_slots[s],
                           gTb[:, d * R + r * 128: d * R + (r + 1) * 128],
                           iwTb[:, d * 1024 + o * 512: d * 1024 + (o + 1) * 512],
                           start=(d == 0), stop=(d == NCH - 1),
                           skip_group_check=(0 < d < NCH - 1)),
                       waits=w_dc)
        for idx in range(4):
            wseries_tail(idx)
            emit_dq(1)
        for idx in range(4, 16):
            rc, oh = divmod(idx, 2)
            slot_i = idx % 4
            waits = [("s_act", evac_vals[idx - 4])]
            for dc in range(NCH):
                w_dc = (waits if dc == 0 else []) + [("s_act", gt_conv[dc])]
                P.emit("tensor",
                       lambda e, r=rc, o=oh, d=dc, s=slot_i: e.matmul(
                           pw_slots[s],
                           gTb[:, d * R + r * 128: d * R + (r + 1) * 128],
                           iwTb[:, d * 1024 + o * 512: d * 1024 + (o + 1) * 512],
                           start=(d == 0), stop=(d == NCH - 1),
                           skip_group_check=(0 < d < NCH - 1)),
                       waits=w_dc)
            mm_done = P.val("s_pe")
            mm_done = wseries_tail(idx)
            if idx == 15:
                P.emit("scalar", lambda e: e.dma_start(
                    xs[:, :].rearrange("p (c t) -> p c t", c=NCH),
                    xk.ap()[:, 0:TG].rearrange("(c p) t -> p c t", p=128)),
                    waits=[("s_pe", mm_done)], inc="s_x0", amt=16)
            emit_dq(2)

        wt_done = P.val("s_act")
        emit_dq(64)

        P.emit("scalar", lambda e: e.activation(
            pool[:, 0:2048], xs[:, :], AF.Copy, scale=1.0),
            waits=[("s_x0", 16)], inc="s_act")
        xb0_done = P.val("s_act")

        blockP = pes.enter_context(nc.Block("prologue"))
        P.replay(blockP, handles)


    pzacc = ps("pzacc", [128, NCH * TG])
    pa = ps("pa", [128, 512])
    pb = ps("pb", [128, 512])
    pst_t = ps("pst", [128, 512])
    pstA = pst_t[0:1, :]
    pstB = pst_t[32:33, :]
    pbc = ps("pbc", [128, 512])

    M = Prog()
    M.cnt = dict(P.cnt)

    def preload6(e):
        return e.add_instruction(mybir.InstLoadActFuncSet(
            name=e.bass.get_next_instruction_name(), act_func_set_id=6,
            ins=[], outs=[]))

    def slot(i):
        return pool[:, (i % 3) * 2048:(i % 3) * 2048 + 2048]

    def slot_sl(i, dc):
        base = (i % 3) * 2048
        return pool[:, base + dc * TG: base + (dc + 1) * TG]

    st = {
        "y1_evac": {}, "x1_done": {}, "h_done": {}, "h_chunks": {},
        "y2_chunks": {}, "zstt_done": {},
        "y2_done": {}, "out_done": {}, "fc1_done": {}, "fc2_done": {},
        "attn_done": {}, "colsq_done": {}, "delta_done": {}, "ln3_rows": {},
        "pstA_free": [("s_act", 0)], "pstB_free": [("s_act", 0)],
        "pbc_free": [("s_dve", 0)], "rowsA_war": 0, "rowsB_war": 0,
        "pst_wr": 0, "pst_rd": {"s_dve": 0, "s_act": 0},
    }
    xb_done = {0: ("s_act", xb0_done)}
    x_loaded = {0: 16, 1: 32, 2: 48}

    def x_load(g, wait_out_count, extra=()):
        w = ([("s_out", 16 * wait_out_count)] if wait_out_count else []) + list(extra)
        M.emit("sync",
               lambda e, gg=g: e.dma_start(
                   xs[:, :].rearrange("p (c t) -> p c t", c=NCH),
                   xk.ap()[:, gg * TG:(gg + 1) * TG].rearrange("(c p) t -> p c t", p=128)),
               waits=w, inc="s_x", amt=16)
        x_loaded[g] = M.val("s_x")

    def xb_convert(g, extra_waits=()):
        w = [("s_x", x_loaded[g])] + list(extra_waits)
        M.emit("vector",
               lambda e, gg=g: e.tensor_copy(slot(gg), xs[:, :]),
               waits=w + [("s_dve", M.val("s_dve"))], inc="s_dve")
        xb_done[g] = ("s_dve", M.val("s_dve"))

    def emit_colsq_sq(g, scr_free_pe, on_act=False):
        w = [xb_done[g], ("s_pe", scr_free_pe)]
        if on_act:
            M.emit("scalar",
                   lambda e, gg=g: e.activation(scr[:, :], slot(gg), AF.Square,
                                                scale=1.0), waits=w, inc="s_act")
            st["colsq_sq"] = ("s_act", M.val("s_act"))
            return
        M.emit("gpsimd",
               lambda e, gg=g: e.tensor_tensor(scr[:, :], slot(gg), slot(gg),
                                               op=OP.mult),
               waits=w, drain=True, inc="s_gp")
        st["colsq_sq"] = ("s_gp", M.val("s_gp"))

    def emit_colsq_stats(g):
        pe0 = stats_mms(lambda d: scr[:, d * TG:(d + 1) * TG],
                        [st["colsq_sq"]], pstA, "pstA_free")
        st["colsq_done"][g] = pe0
        M.emit("scalar", lambda e: e.activation(delta_b[0:1, :], pstA[0:1, 0:TG], AF.Copy,
                                                scale=-0.5, bias=512.0),
               waits=[("s_pe", pe0)], inc="s_act")
        st["delta_done"][g] = M.val("s_act")
        st["pstA_free"] = [("s_act", st["delta_done"][g])]
        st["pst_rd"]["s_act"] = st["delta_done"][g]

    def dv(fn, waits=()):
        M.emit("vector", fn,
               waits=list(waits) + [("s_dve", M.val("s_dve"))], inc="s_dve")

    D_ = float(D)

    def row_front_A(rs, s_mm, war_pe):
        dv(lambda e: e.tensor_scalar(rwsb[0:1, :], pstA[0:1, :], 1.0 / D_, None,
                                     op0=OP.mult),
           waits=[("s_pe", max(s_mm, st["pst_wr"])), ("s_act", M.val("s_act"))])
        st["pstA_free"] = [("s_dve", M.val("s_dve"))]
        st["pst_rd"]["s_dve"] = M.val("s_dve")
        dv(lambda e: e.tensor_tensor(ry[0:1, :], rwsb[0:1, 0:TG], rwsb[0:1, 0:TG],
                                     op=OP.mult))
        dv(lambda e: e.scalar_tensor_tensor(rwsb[0:1, TG:2 * TG], rwsb[0:1, TG:2 * TG],
                                            1e-5, ry[0:1, :],
                                            op0=OP.add, op1=OP.subtract))
        dv(lambda e: e.tensor_copy(rs["nmu"][0:1, :], rwsb[0:1, 0:TG]),
           waits=[("s_pe", war_pe)])
        return M.val("s_dve")

    def row_front_B(rs, s_mm, war_pe):
        dv(lambda e: e.tensor_scalar(rwsb[0:1, 0:TG], pstB[0:1, 0:TG], 1.0 / D_, None,
                                     op0=OP.mult),
           waits=[("s_pe", max(s_mm, st["pst_wr"])), ("s_act", M.val("s_act"))])
        dv(lambda e: e.tensor_copy(rs["nmu"][0:1, :], rwsb[0:1, 0:TG]),
           waits=[("s_pe", war_pe)])
        dv(lambda e: e.tensor_tensor(rwsb[0:1, 0:TG], rwsb[0:1, 0:TG], rwsb[0:1, 0:TG],
                                     op=OP.mult))
        dv(lambda e: e.tensor_scalar(ry[0:1, :], pstB[0:1, TG:2 * TG], 1.0 / D_, None,
                                     op0=OP.mult))
        st["pstB_free"] = [("s_dve", M.val("s_dve"))]
        st["pst_rd"]["s_dve"] = M.val("s_dve")
        dv(lambda e: e.scalar_tensor_tensor(ry[0:1, :], ry[0:1, :], 1e-5,
                                            rwsb[0:1, 0:TG],
                                            op0=OP.add, op1=OP.subtract))
        return M.val("s_dve")

    def row_front_V(rs, s_mm):
        dv(lambda e: e.tensor_scalar(ry[0:1, :], pstB[0:1, TG:2 * TG], 1.0 / D_,
                                     1e-5, op0=OP.mult, op1=OP.add),
           waits=[("s_pe", max(s_mm, st["pst_wr"])), ("s_act", M.val("s_act"))])
        st["pstB_free"] = [("s_dve", M.val("s_dve"))]
        st["pst_rd"]["s_dve"] = M.val("s_dve")
        return M.val("s_dve")

    def row_tail(rs, var_ap, widen_ap, front_val):
        M.emit("scalar", lambda e: e.activation(var_ap, var_ap, AF.Ln, scale=1.0),
               waits=[("s_dve", front_val), ("s_act", M.val("s_act"))], inc="s_act")
        M.emit("scalar", lambda e: e.activation(var_ap, var_ap, AF.Exp, scale=-0.5),
               waits=[("s_act", M.val("s_act"))], inc="s_act")
        dv(lambda e: e.tensor_copy(rs["rh"][0:1, :], var_ap),
           waits=[("s_act", M.val("s_act"))])
        dv(lambda e: e.tensor_copy(widen_ap, rs["rh"][0:1, :]))
        dv(lambda e: e.tensor_tensor(widen_ap, var_ap, widen_ap, op=OP.subtract))
        dv(lambda e: e.tensor_copy(rs["rl"][0:1, :], widen_ap))
        return M.val("s_dve")

    varA = rwsb[0:1, TG:2 * TG]
    varB = ry[0:1, :]
    widen = rwsb[0:1, 0:TG]

    def stats_mms(src_slices, wait_list, pslot, pkey):
        for dc in range(NCH):
            w = (wait_list + st[pkey] + list(st["pst_rd"].items())) if dc == 0 else ()
            M.emit("tensor",
                   lambda e, d=dc: e.matmul(pslot[0:1, 0:TG], ones[:, 0:1],
                                            src_slices(d),
                                            start=(d == 0), stop=(d == NCH - 1)),
                   waits=w, inc="s_pe" if dc == NCH - 1 else None)
        st["pst_wr"] = M.val("s_pe")
        return M.val("s_pe")

    def stats2_mms(src_slices, wait_list, pslot):
        for dc in range(NCH):
            w = (wait_list + list(st["pst_rd"].items())) if dc == 0 else ()
            M.emit("tensor",
                   lambda e, d=dc: e.matmul(pslot[0:1, TG:2 * TG], ones[:, 0:1],
                                            src_slices(d),
                                            start=False, stop=(d == NCH - 1),
                                            skip_group_check=True),
                   waits=w, inc="s_pe" if dc == NCH - 1 else None)
        st["pst_wr"] = M.val("s_pe")
        return M.val("s_pe")

    def stats2a_mms(src_slices, wait_list, pslot):
        for dc in range(NCH):
            w = (wait_list + st["pstB_free"]
                 + list(st["pst_rd"].items())) if dc == 0 else ()
            M.emit("tensor",
                   lambda e, d=dc: e.matmul(pslot[0:1, TG:2 * TG], ones[:, 0:1],
                                            src_slices(d),
                                            start=(d == 0), stop=(d == NCH - 1)),
                   waits=w, inc="s_pe" if dc == NCH - 1 else None)
        st["pst_wr"] = M.val("s_pe")
        return M.val("s_pe")

    def bc2V_mms(rows_ready, rs):
        M.emit("tensor", lambda e: e.matmul(pbc[:, TG:2 * TG], ones[0:1, 0:128], rs["rh"][0:1, :],
                                            start=True, stop=True),
               waits=[("s_dve", rows_ready)] + st["pbc_free"])
        M.emit("tensor", lambda e: e.matmul(pbc[:, TG:2 * TG], ones[0:1, 0:128], rs["rl"][0:1, :],
                                            start=False, stop=False, skip_group_check=True),
               inc="s_pe")
        return M.val("s_pe")

    def bc_mms(rows_ready, rs):
        M.emit("tensor", lambda e: e.matmul(pbc[:, 0:TG], ones[0:1, 0:128], rs["nmu"][0:1, :],
                                            start=True, stop=True),
               waits=[("s_dve", rows_ready)] + st["pbc_free"])
        M.emit("tensor", lambda e: e.matmul(pbc[:, TG:2 * TG], ones[0:1, 0:128], rs["rh"][0:1, :],
                                            start=False, stop=False, skip_group_check=True))
        M.emit("tensor", lambda e: e.matmul(pbc[:, TG:2 * TG], ones[0:1, 0:128], rs["rl"][0:1, :],
                                            start=False, stop=False, skip_group_check=True),
               inc="s_pe")
        return M.val("s_pe")

    def emit_X(g, wu=None):
        delta_done = st["delta_done"][g]
        fe_evac = {}

        def emit_mains(rc):
            bank = rc % 2
            pwork = pa if bank == 0 else pb
            silu_free = st.get("silu_pa" if bank == 0 else "silu_pb", 0)
            w = [xb_done[g], ("s_act", silu_free)]
            if rc >= 2:
                w = [("s_act", fe_evac[rc - 2])]
            for dc in range(NCH):
                M.emit("tensor",
                       lambda e, r=rc, d=dc, p=pwork: e.matmul(
                           p[:, 0:TG],
                           gTb[:, d * R + r * 128: d * R + (r + 1) * 128],
                           slot_sl(g, d),
                           start=(d == 0), stop=False),
                       waits=w if dc == 0 else ())

        def emit_corr_evac(rc, delta_done):
            bank = rc % 2
            pwork = pa if bank == 0 else pb
            M.emit("tensor",
                   lambda e, r=rc, p=pwork: e.matmul(
                       p[:, 0:TG], ab_row[0:1, r * 128:(r + 1) * 128], delta_b[0:1, :],
                       start=False, stop=True),
                   waits=[("s_act", delta_done)], inc="s_pe")
            mm_done = M.val("s_pe")
            M.emit("scalar",
                   lambda e, r=rc, p=pwork: e.activation(
                       fey[:, (r % 2) * TG:(r % 2 + 1) * TG], p[:, 0:TG], AF.Exp,
                       bias=bias_f[:, r:r + 1], scale=xis_pp[:, r:r + 1]),
                   waits=[("s_pe", mm_done)], inc="s_act")
            fe_evac[rc] = M.val("s_act")

        def emit_attn(rc):
            w = [("s_act", fe_evac[rc])]
            if rc == 0:
                w.append(("s_dve", max(st["zstt_done"].get(g - 2, 0),
                                       st["y1_evac"].get(g - 1, 0))))
            for oc in range(NCH):
                first = (rc == 0 and oc % 2 == 0)
                last = (rc == NCH - 1 and oc % 2 == 1)
                M.emit("tensor",
                       lambda e, r=rc, o=oc, fi=first, la=last: e.matmul(
                           pzacc[:, o * TG:(o + 1) * TG],
                           WT[:, r * D + o * 128: r * D + (o + 1) * 128],
                           fey[:, (r % 2) * TG:(r % 2 + 1) * TG],
                           start=fi, stop=la,
                           skip_group_check=not (fi or la)),
                       waits=w if oc == 0 else (),
                       inc="s_pe" if (rc == NCH - 1 and oc == NCH - 1) else None)

        emit_mains(0)
        emit_mains(1)
        emit_corr_evac(0, delta_done)
        emit_corr_evac(1, delta_done)
        for rc in range(NCH):
            emit_attn(rc)
            if rc + 2 < NCH:
                emit_mains(rc + 2)
                emit_corr_evac(rc + 2, delta_done)
            if rc == 1:
                consume_defer_h()
            if wu and rc in wu:
                wu[rc]()
        st["attn_done"][g] = M.val("s_pe")
        if wu and 8 in wu:
            wu[8]()

        for oc in range(NCH):
            w = [("s_pe", st["attn_done"][g])] if oc == 0 else ()
            M.emit("vector",
                   lambda e, o=oc, gg=g: e.tensor_tensor(
                       slot_sl(gg, o), slot_sl(gg, o), pzacc[:, o * TG:(o + 1) * TG],
                       op=OP.add),
                   waits=w, drain=(oc == 0 or oc == NCH - 1),
                   inc="s_dve" if oc == NCH - 1 else None)
        st["y1_evac"][g] = M.val("s_dve")
        if g >= 2:
            M.emit("scalar", lambda e: e.activation(dmy[0:1, 0:1], dmy[0:1, 0:1],
                                                    AF.Silu, scale=1.0))

    def emit_LN3_copy(og, war=None, fine=False):
        if og < 0:
            return
        warv = war if war is not None else M.val("s_pe")
        if not fine:
            w = [("s_pe", warv), ("s_gp", st["y2_done"][og])]
            M.emit("scalar",
                   lambda e: e.activation(scr[:, :], xs[:, :], AF.Copy,
                                          scale=1.0), waits=w, inc="s_act")
            st["ln3_cp"] = ("s_act", M.val("s_act"))
            return
        qs = []
        for q in range(4):
            w = [("s_gp", st["y2_chunks"][og][2 * q + 1])]
            if q == 0:
                w.append(("s_pe", warv))
            M.emit("scalar",
                   lambda e, qq=q: e.activation(
                       scr[:, qq * 2 * TG:(qq + 1) * 2 * TG],
                       xs[:, qq * 2 * TG:(qq + 1) * 2 * TG],
                       AF.Copy, scale=1.0), waits=w, inc="s_act")
            qs.append(M.val("s_act"))
        st["ln3_cp_q"] = qs
        st["ln3_cp"] = ("s_act", qs[-1])

    def emit_LN3_s1(og, fine=False):
        if og < 0:
            return
        s1ch = []
        for dc in range(NCH):
            w = (st["pstA_free"] + list(st["pst_rd"].items())) if dc == 0 else []
            if fine:
                if dc % 2 == 0:
                    w = w + [("s_act", st["ln3_cp_q"][dc // 2])]
            elif dc == 0:
                w = w + [st["ln3_cp"]]
            M.emit("tensor",
                   lambda e, d=dc: e.matmul(pstA[0:1, 0:TG], ones[:, 0:1],
                                            scr[:, d * TG:(d + 1) * TG],
                                            start=(d == 0), stop=(d == NCH - 1)),
                   waits=w, inc="s_pe" if (fine or dc == NCH - 1) else None)
            if fine:
                s1ch.append(M.val("s_pe"))
        st["ln3_s1_ch"] = s1ch
        st["pst_wr"] = M.val("s_pe")

    def emit_LN3_sqr(og, on_act=False, fine=False):
        if og < 0:
            return
        if fine:
            hs = []
            for hh in range(2):
                M.emit("scalar",
                       lambda e, h2=hh: e.activation(
                           scr[:, h2 * 4 * TG:(h2 + 1) * 4 * TG],
                           scr[:, h2 * 4 * TG:(h2 + 1) * 4 * TG],
                           AF.Square, scale=1.0),
                       waits=[("s_pe", st["ln3_s1_ch"][4 * hh + 3])], inc="s_act")
                hs.append(M.val("s_act"))
            st["ln3_sq_h"] = hs
            st["ln3_sq"] = ("s_act", hs[-1])
            return
        w = [("s_pe", M.val("s_pe"))]
        if on_act:
            M.emit("scalar",
                   lambda e: e.activation(scr[:, :], scr[:, :], AF.Square,
                                          scale=1.0), waits=w, inc="s_act")
            st["ln3_sq"] = ("s_act", M.val("s_act"))
            return
        M.emit("gpsimd",
               lambda e: e.tensor_tensor(scr[:, :], scr[:, :], scr[:, :], op=OP.mult),
               waits=w, drain=True, inc="s_gp")
        st["ln3_sq"] = ("s_gp", M.val("s_gp"))

    def emit_LN3_mid(og, fine=False):
        if og < 0:
            return
        if not fine:
            st["ln3_s2"] = stats2_mms(lambda d: scr[:, d * TG:(d + 1) * TG],
                                      [st["ln3_sq"]], pstA)
            return
        for dc in range(NCH):
            w = list(st["pst_rd"].items()) if dc == 0 else []
            if dc % 4 == 0:
                w = w + [("s_act", st["ln3_sq_h"][dc // 4])]
            M.emit("tensor",
                   lambda e, d=dc: e.matmul(pstA[0:1, TG:2 * TG], ones[:, 0:1],
                                            scr[:, d * TG:(d + 1) * TG],
                                            start=False, stop=(d == NCH - 1),
                                            skip_group_check=True),
                   waits=w, inc="s_pe" if dc == NCH - 1 else None)
        st["pst_wr"] = M.val("s_pe")
        st["ln3_s2"] = M.val("s_pe")

    def emit_LN3_B(og):
        if og < 0:
            return
        bc = bc_mms(st["ln3_rows"][og], rowsA)
        st["rowsA_war"] = bc
        oq = {}
        for dc in range(NCH):
            w2 = [("s_pe", bc)] if dc == 0 else ()
            M.emit("vector",
                   lambda e, d=dc: e.tensor_tensor(xs[:, d * TG:(d + 1) * TG],
                                                   xs[:, d * TG:(d + 1) * TG],
                                                   pbc[:, 0:TG], op=OP.subtract),
                   waits=w2, drain=(dc == 0))
            M.emit("vector",
                   lambda e, d=dc: e.tensor_tensor(xs[:, d * TG:(d + 1) * TG],
                                                   xs[:, d * TG:(d + 1) * TG],
                                                   pbc[:, TG:2 * TG], op=OP.mult),
                   drain=True,
                   inc="s_dve" if dc % 2 == 1 else None)
            if dc % 2 == 1:
                oq[dc // 2] = M.val("s_dve")
        st["out_q"] = oq
        st["out_done"][og] = M.val("s_dve")
        st["pbc_free"] = [("s_dve", st["out_done"][og])]
        if og == NG - 1:
            for q in range(4):
                M.emit("sync" if q % 2 == 0 else "scalar",
                       lambda e, gg=og, qq=q: e.dma_start(
                           out.ap()[qq * 256:(qq + 1) * 256,
                                    gg * TG:(gg + 1) * TG].rearrange(
                               "(c p) t -> p c t", p=128),
                           xs[:, qq * 2 * TG:(qq + 1) * 2 * TG].rearrange(
                               "p (c t) -> p c t", c=2)),
                       waits=[("s_dve", st["out_q"][q])], inc="s_out", amt=16)
        else:
            M.emit("sync",
                   lambda e, gg=og: e.dma_start(
                       out.ap()[:, gg * TG:(gg + 1) * TG].rearrange("(c p) t -> p c t", p=128),
                       xs[:, :].rearrange("p (c t) -> p c t", c=NCH)),
                   waits=[("s_dve", st["out_done"][og])], inc="s_out", amt=16)
        if og + 3 < NG:
            x_load(og + 3, og + 1)

    def emit_ln1_stats(lg):
        return stats_mms(lambda d: slot_sl(lg, d), [("s_dve", st["y1_evac"][lg])],
                         pstB, "pstB_free")

    def emit_sq(lg, on_act=False):
        w = [("s_dve", st["y1_evac"][lg]), ("s_pe", M.val("s_pe"))]
        if on_act:
            M.emit("scalar",
                   lambda e, gg=lg: e.activation(scr[:, :], slot(gg), AF.Square,
                                                 scale=1.0), waits=w, inc="s_act")
            return ("s_act", M.val("s_act"))
        M.emit("gpsimd",
               lambda e, gg=lg: e.tensor_tensor(scr[:, :], slot(gg), slot(gg),
                                               op=OP.mult),
               waits=w, drain=True, inc="s_gp")
        return ("s_gp", M.val("s_gp"))

    def emit_sq2(lg, on_act=False):
        w = [("s_dve", st["x1_done"][lg]), ("s_pe", M.val("s_pe"))]
        if on_act:
            M.emit("scalar",
                   lambda e, gg=lg: e.activation(scr[:, :], slot(gg), AF.Square,
                                                 scale=1.0), waits=w, inc="s_act")
            return ("s_act", M.val("s_act"))
        M.emit("gpsimd",
               lambda e, gg=lg: e.tensor_tensor(scr[:, :], slot(gg), slot(gg),
                                               op=OP.mult),
               waits=w, drain=True, inc="s_gp")
        return ("s_gp", M.val("s_gp"))

    def emit_norm_stream(lg, bc):
        for dc in range(NCH):
            w = [("s_pe", bc)] if dc == 0 else ()
            M.emit("vector",
                   lambda e, d=dc, gg=lg: e.tensor_tensor(
                       slot_sl(gg, d), slot_sl(gg, d), pbc[:, 0:TG], op=OP.subtract),
                   waits=w, drain=(dc == 0))
        for dc in range(NCH):
            M.emit("vector",
                   lambda e, d=dc, gg=lg: e.tensor_tensor(
                       slot_sl(gg, d), slot_sl(gg, d), pbc[:, TG:2 * TG], op=OP.mult),
                   drain=(dc == 0 or dc == NCH - 1),
                   inc="s_dve" if dc == NCH - 1 else None)
        return M.val("s_dve")

    def emit_h(lg, fc1_wait, bc2, dve_gb=False):
        if dve_gb:
            chunks = []
            for dc in range(NCH):
                w = [("s_pe", max(fc1_wait, bc2))] if dc == 0 else ()
                M.emit("vector",
                       lambda e, d=dc, gg=lg: e.tensor_tensor(
                           h_b[:, d * TG:(d + 1) * TG], slot_sl(gg, d),
                           pbc[:, TG:2 * TG], op=OP.mult),
                       waits=w, drain=(dc == 0))
                M.emit("vector",
                       lambda e, d=dc: e.tensor_scalar(
                           h_b[:, d * TG:(d + 1) * TG], h_b[:, d * TG:(d + 1) * TG],
                           gam[:, d:d + 1], bet[:, d:d + 1], op0=OP.mult, op1=OP.add),
                       drain=True, inc="s_dve")
                chunks.append(M.val("s_dve"))
            st["h_chunks"][lg] = chunks
            st["h_mul_done"] = chunks[-1]
            st["h_done"][lg] = ("s_dve", M.val("s_dve"))
            return
        mul_vals = []
        for dc in range(NCH):
            w = [("s_pe", max(fc1_wait, bc2))] if dc == 0 else ()
            M.emit("vector",
                   lambda e, d=dc, gg=lg: e.tensor_tensor(
                       h_b[:, d * TG:(d + 1) * TG], slot_sl(gg, d),
                       pbc[:, TG:2 * TG], op=OP.mult),
                   waits=w, drain=(dc == 0), inc="s_dve")
            mul_vals.append(M.val("s_dve"))
        st["h_mul_done"] = mul_vals[-1]
        for dc in range(NCH):
            M.emit("gpsimd",
                   lambda e, d=dc: e.tensor_scalar(
                       h_b[:, d * TG:(d + 1) * TG], h_b[:, d * TG:(d + 1) * TG],
                       gam[:, d:d + 1], bet[:, d:d + 1], op0=OP.mult, op1=OP.add),
                   waits=[("s_dve", mul_vals[dc])],
                   drain=(dc == NCH - 1), inc="s_gp" if dc == NCH - 1 else None)
        st["h_done"][lg] = ("s_gp", M.val("s_gp"))

    def warmup_hooks(lg):
        wst = {}
        def h0():
            wst["s1"] = emit_ln1_stats(lg)
        def h1():
            wst["sq"] = emit_sq(lg, on_act=True)
        def h2():
            wst["s2"] = stats2_mms(lambda d: scr[:, d * TG:(d + 1) * TG],
                                   [wst["sq"]], pstB)
            wst["fr1"] = row_front_B(rowsB, wst["s2"], st["rowsB_war"])
        def h3():
            wst["rows1"] = row_tail(rowsB, varB, widen, wst["fr1"])
        def h4():
            bc1 = bc_mms(wst["rows1"], rowsB)
            wst["bc1"] = bc1
            x1d = emit_norm_stream(lg, bc1)
            st["x1_done"][lg] = x1d
            st["pbc_free"] = [("s_dve", x1d)]
        def h8():
            sqb = emit_sq2(lg, on_act=True)
            wst["s2b"] = stats2a_mms(lambda d: scr[:, d * TG:(d + 1) * TG],
                                     [sqb], pstB)
            wst["fr2"] = row_front_V(rowsB, wst["s2b"])
        return wst, {0: h0, 1: h1, 2: h2, 3: h3, 4: h4, 8: h8}

    def warmup_tail(lg, wst):
        rows2 = row_tail(rowsB, varB, widen, wst["fr2"])
        bc2 = bc2V_mms(rows2, rowsB)
        st["rowsB_war"] = bc2
        emit_h(lg, st["fc1_done"].get(lg - 1, 0), bc2, dve_gb=True)
        st["pbc_free"] = [("s_dve", st["h_mul_done"])]
        M.emit("scalar", lambda e: e.activation(dmy[0:1, 0:1], dmy[0:1, 0:1],
                                                AF.Silu, scale=1.0))

    def emit_zevac(fg, fc2_stop):
        xs_w = [("s_pe", fc2_stop), ("s_dve", st["x1_done"][fg])]
        if fg + 2 < NG:
            xs_w.append(xb_done[fg + 2])
        else:
            xs_w.append(("s_out", 16 * fg))
        z_vals = []
        for oc in range(NCH):
            w = xs_w if oc == 0 else ()
            M.emit("vector",
                   lambda e, o=oc, gg=fg: e.scalar_tensor_tensor(
                       xs[:, o * TG:(o + 1) * TG], pzacc[:, o * TG:(o + 1) * TG], 0.5,
                       slot_sl(gg, o), op0=OP.mult, op1=OP.add),
                   waits=w, drain=(oc == 0 or oc == NCH - 1), inc="s_dve")
            z_vals.append(M.val("s_dve"))
        st["zstt_done"][fg] = z_vals[-1]
        y2c = []
        for oc in range(NCH):
            M.emit("gpsimd",
                   lambda e, o=oc: e.tensor_scalar(
                       xs[:, o * TG:(o + 1) * TG], xs[:, o * TG:(o + 1) * TG],
                       onesc[:, 0:1], b2h[:, o:o + 1], op0=OP.mult, op1=OP.add),
                   waits=[("s_dve", z_vals[oc])],
                   drain=(oc == NCH - 1), inc="s_gp")
            y2c.append(M.val("s_gp"))
        st["y2_chunks"][fg] = y2c
        st["y2_done"][fg] = M.val("s_gp")

    def consume_defer_h():
        d = st.pop("defer_h", None)
        if d is None:
            return
        lg_, fc1a, r2 = d
        bc2 = bc2V_mms(r2, rowsB)
        st["rowsB_war"] = bc2
        emit_h(lg_, fc1a, bc2, dve_gb=True)
        st["pbc_free"] = [("s_dve", st["h_mul_done"])]

    def emit_F(fg, ln3_og, copy_og=None):
        lg = fg + 1
        do_ln = lg < NG
        scratch = lg + 1
        silu_done = {}
        ctx = {}

        def emit_fc1(fc):
            bank = fc % 2
            pwork = pa if bank == 0 else pb
            w = [st["h_done"][fg]]
            if fc >= 2:
                w.append(("s_act", silu_done[fc - 2]))
            hch = st["h_chunks"].get(fg) if fc == 0 else None
            for hc in range(NCH):
                if hch is not None:
                    wc = [("s_dve", hch[hc])]
                elif hc == 0:
                    wc = w
                else:
                    wc = ()
                M.emit("tensor",
                       lambda e, f=fc, h=hc, p=pwork: e.matmul(
                           p[:, 0:TG],
                           w1b[:, h * F + f * 128: h * F + (f + 1) * 128],
                           h_b[:, h * TG:(h + 1) * TG],
                           start=(h == 0), stop=(h == NCH - 1)),
                       waits=wc, inc="s_pe" if hc == NCH - 1 else None)
            mm_done = M.val("s_pe")
            M.emit("scalar",
                   lambda e, f=fc, p=pwork: e.activation(
                       fey[:, (f % 4) * TG:(f % 4 + 1) * TG], p[:, 0:TG], AF.Silu,
                       bias=b1c[:, f:f + 1], scale=1.0),
                   waits=[("s_pe", mm_done)], inc="s_act")
            silu_done[fc] = M.val("s_act")

        def emit_fc2(FB):
            for j in range(4):
                fc = FB * 4 + j
                sd = silu_done[fc]
                for oc in range(NCH):
                    first = (FB == 0 and j == 0 and oc % 2 == 0)
                    last = (FB == 7 and j == 3 and oc % 2 == 1)
                    w = [("s_act", sd)]
                    if first and oc == 0:
                        w.append(("s_dve", max(st["y1_evac"][min(fg + 1, NG - 1)],
                                               st["zstt_done"].get(fg - 1, 0))))
                    M.emit("tensor",
                           lambda e, f=fc, o=oc, jj=j, fi=first, la=last: e.matmul(
                               pzacc[:, o * TG:(o + 1) * TG],
                               w2b[:, f * D + o * 128: f * D + (o + 1) * 128],
                               fey[:, jj * TG:(jj + 1) * TG],
                               start=fi, stop=la,
                               skip_group_check=not (fi or la)),
                           waits=w if oc == 0 else (),
                           inc="s_pe" if (FB == 7 and oc == NCH - 1 and j == 3) else None)

        for FB in range(8):
            for j in range(4):
                emit_fc1(FB * 4 + j)
            if FB == 7:
                ctx["fc1_tail"] = M.val("s_pe")
            emit_fc2(FB)
            if FB == 0:
                if copy_og is not None:
                    emit_LN3_copy(copy_og)
                emit_LN3_s1(ln3_og)
                if do_ln:
                    ctx["s1"] = emit_ln1_stats(lg)
            if FB == 1:
                emit_LN3_sqr(ln3_og, on_act=True)
                emit_LN3_mid(ln3_og)
                if ln3_og >= 0:
                    ctx["frA"] = row_front_A(rowsA, st["ln3_s2"], st["rowsA_war"])
            if FB == 2:
                if do_ln:
                    ctx["sq"] = emit_sq(lg, on_act=True)
                    ctx["s2_1"] = stats2_mms(lambda d: scr[:, d * TG:(d + 1) * TG],
                                             [ctx["sq"]], pstB)
                if ln3_og >= 0:
                    st["ln3_rows"][ln3_og] = row_tail(rowsA, varA, widen, ctx["frA"])
                    M.emit("scalar", lambda e: e.activation(
                        dmy[0:1, 0:1], dmy[0:1, 0:1], AF.Silu, scale=1.0))
                if do_ln:
                    ctx["frB"] = row_front_B(rowsB, ctx["s2_1"], st["rowsB_war"])
            if FB == 3:
                if do_ln:
                    M.emit("scalar", lambda e: e.activation(
                        dmy[0:1, 0:1], onesc[0:1, 0:1], AF.Ln, scale=1.0))
                    ctx["rows1"] = row_tail(rowsB, varB, widen, ctx["frB"])
                    M.emit("scalar", lambda e: e.activation(
                        dmy[0:1, 0:1], dmy[0:1, 0:1], AF.Silu, scale=1.0))
                emit_LN3_B(ln3_og)
            if FB == 4 and do_ln:
                ctx["bc1"] = bc_mms(ctx["rows1"], rowsB)
                x1d = emit_norm_stream(lg, ctx["bc1"])
                st["x1_done"][lg] = x1d
                st["pbc_free"] = [("s_dve", x1d)]
            if FB == 5:
                if do_ln:
                    if lg == NG - 1:
                        ctx["sqb"] = emit_sq2(lg, on_act=True)
                        ctx["s2b"] = stats2a_mms(
                            lambda d: scr[:, d * TG:(d + 1) * TG],
                            [ctx["sqb"]], pstB)
                        ctx["fr2"] = row_front_V(rowsB, ctx["s2b"])
                        M.emit("scalar", lambda e: e.activation(
                            dmy[0:1, 0:1], onesc[0:1, 0:1], AF.Ln, scale=1.0))
                        ctx["rows2"] = row_tail(rowsB, varB, widen, ctx["fr2"])
                        M.emit("scalar", lambda e: e.activation(
                            dmy[0:1, 0:1], dmy[0:1, 0:1], AF.Silu, scale=1.0))
                    else:
                        ctx["sqb"] = emit_sq2(lg)
                if 3 <= fg + 2 < NG:
                    xb_convert(fg + 2,
                               extra_waits=[("s_gp", st["y2_done"].get(fg - 1, 0))])
            if FB == 6:
                if do_ln:
                    if lg == NG - 1:
                        ctx["bc2f"] = bc2V_mms(ctx["rows2"], rowsB)
                    else:
                        ctx["s2b"] = stats2a_mms(lambda d: scr[:, d * TG:(d + 1) * TG],
                                                 [ctx["sqb"]], pstB)
                        ctx["fr2"] = row_front_V(rowsB, ctx["s2b"])
                if fg + 2 < NG:
                    emit_colsq_sq(fg + 2, M.val("s_pe"))
            if FB == 7:
                fc2_stop = M.val("s_pe")
                if do_ln and lg == NG - 1:
                    st["rowsB_war"] = ctx["bc2f"]
                    emit_h(lg, ctx["fc1_tail"], ctx["bc2f"], dve_gb=True)
                    st["pbc_free"] = [("s_dve", st["h_mul_done"])]
                emit_zevac(fg, fc2_stop)
                if do_ln:
                    if lg == NG - 1:
                        pass
                    else:
                        ctx["rows2"] = row_tail(rowsB, varB, widen, ctx["fr2"])
                        st["defer_h"] = (lg, ctx["fc1_tail"], ctx["rows2"])
                if FB == 7 and fg + 2 < NG:
                    emit_colsq_stats(fg + 2)
        st["fc1_done"][fg] = M.val("s_pe")
        st["fc2_done"][fg] = M.val("s_pe")
        st["silu_last"] = M.val("s_act")
        st["silu_pa"] = silu_done[30]
        st["silu_pb"] = silu_done[31]

    emit_colsq_sq(0, M.val("s_pe"), on_act=True)
    emit_colsq_stats(0)
    M.emit("scalar", lambda e: e.dma_start(
        xs[:, :].rearrange("p (c t) -> p c t", c=NCH),
        xk.ap()[:, TG:2 * TG].rearrange("(c p) t -> p c t", p=128)),
        waits=[("s_act", xb0_done)], inc="s_x0", amt=16)
    M.emit("vector", lambda e: e.tensor_copy(pool[:, 2048:4096], xs[:, :]),
           waits=[("s_x0", M.val("s_x0")), ("s_dve", M.val("s_dve"))],
           inc="s_dve")
    xb_done[1] = ("s_dve", M.val("s_dve"))

    for g in range(NG):
        if g == 0:
            emit_X(g, wu={5: lambda: emit_colsq_sq(1, st["colsq_done"][0],
                                                   on_act=True),
                          7: lambda: emit_colsq_stats(1)})
            M.emit("scalar", lambda e: e.dma_start(
                xs[:, :].rearrange("p (c t) -> p c t", c=NCH),
                xk.ap()[:, 2 * TG:3 * TG].rearrange("(c p) t -> p c t", p=128)),
                waits=[xb_done[1]], inc="s_x0", amt=16)
            M.emit("vector", lambda e: e.tensor_copy(pool[:, 4096:6144], xs[:, :]),
                   waits=[("s_x0", M.val("s_x0")), ("s_dve", M.val("s_dve"))],
                   inc="s_dve")
            xb_done[2] = ("s_dve", M.val("s_dve"))
        elif g == 1:
            wst, wuh = warmup_hooks(0)
            emit_X(g, wu=wuh)
            warmup_tail(0, wst)
        else:
            emit_X(g)
        emit_LN3_copy(g - 2, war=st["colsq_done"].get(g))
        if g >= 1:
            emit_F(g - 1, g - 2)
    consume_defer_h()
    emit_F(NG - 1, NG - 2, copy_og=NG - 2)
    emit_LN3_copy(NG - 1, fine=True)
    emit_LN3_s1(NG - 1, fine=True)
    emit_LN3_sqr(NG - 1, fine=True)
    M.emit("scalar", lambda e: e.activation(dmy[0:1, 0:1], onesc[0:1, 0:1],
                                            AF.Ln, scale=1.0))
    emit_LN3_mid(NG - 1, fine=True)
    frA_t = row_front_A(rowsA, st["ln3_s2"], st["rowsA_war"])
    M.emit("tensor", lambda e: e.matmul(pbc[:, 0:TG], ones[0:1, 0:128],
                                        rowsA["nmu"][0:1, :],
                                        start=True, stop=True),
           waits=[("s_dve", frA_t)] + st["pbc_free"], inc="s_pe")
    bc_mu = M.val("s_pe")
    for dc in range(NCH):
        M.emit("vector",
               lambda e, d=dc: e.tensor_tensor(xs[:, d * TG:(d + 1) * TG],
                                               xs[:, d * TG:(d + 1) * TG],
                                               pbc[:, 0:TG], op=OP.subtract),
               waits=[("s_pe", bc_mu)] if dc == 0 else (), drain=(dc == 0),
               inc="s_dve" if dc == NCH - 1 else None)
    subs_d = M.val("s_dve")
    st["ln3_rows"][NG - 1] = row_tail(rowsA, varA, widen, frA_t)
    M.emit("tensor", lambda e: e.matmul(pbc[:, TG:2 * TG], ones[0:1, 0:128],
                                        rowsA["rh"][0:1, :],
                                        start=False, stop=False, skip_group_check=True),
           waits=[("s_dve", max(subs_d, st["ln3_rows"][NG - 1]))])
    M.emit("tensor", lambda e: e.matmul(pbc[:, TG:2 * TG], ones[0:1, 0:128],
                                        rowsA["rl"][0:1, :],
                                        start=False, stop=False, skip_group_check=True),
           inc="s_pe")
    bc_f = M.val("s_pe")
    st["rowsA_war"] = bc_f
    oqf = {}
    for dc in range(NCH):
        M.emit("vector",
               lambda e, d=dc: e.tensor_tensor(xs[:, d * TG:(d + 1) * TG],
                                               xs[:, d * TG:(d + 1) * TG],
                                               pbc[:, TG:2 * TG], op=OP.mult),
               waits=[("s_pe", bc_f)] if dc == 0 else (),
               drain=(dc == 0 or dc == NCH - 1),
               inc="s_dve" if dc % 2 == 1 else None)
        if dc % 2 == 1:
            oqf[dc // 2] = M.val("s_dve")
    st["out_done"][NG - 1] = M.val("s_dve")
    st["pbc_free"] = [("s_dve", st["out_done"][NG - 1])]
    for q in range(4):
        M.emit("sync" if q % 2 == 0 else "scalar",
               lambda e, qq=q: e.dma_start(
                   out.ap()[qq * 256:(qq + 1) * 256,
                            (NG - 1) * TG:NG * TG].rearrange(
                       "(c p) t -> p c t", p=128),
                   xs[:, qq * 2 * TG:(qq + 1) * 2 * TG].rearrange(
                       "p (c t) -> p c t", c=2)),
               waits=[("s_dve", oqf[q])], inc="s_out", amt=16)

    M.emit("scalar", None, waits=[("s_out", M.val("s_out"))])

    blockM = es.enter_context(nc.Block("main"))
    M.replay(blockM, handles)
    es.close()
    return nc


def _selector(k):
    import ml_dtypes
    s = np.zeros((k, k * 128), dtype=ml_dtypes.bfloat16)
    for c in range(k):
        s[c, c * 128:(c + 1) * 128] = 1
    return s


_NC_CACHE = {}


def _get_nc():
    if "nc" not in _NC_CACHE:
        _NC_CACHE["nc"] = build()
    return _NC_CACHE["nc"]


def kernel(L0_attn_x, L0_attn_input_weights, L0_attn_xis, L0_attn_proj_matrix,
           L0_ffn_ln_gamma, L0_ffn_ln_beta, L0_ffn_fc1_q, L0_ffn_fc1_scales,
           L0_ffn_fc1_bias, L0_ffn_fc2_q, L0_ffn_fc2_scales, L0_ffn_fc2_bias):
    c = np.ascontiguousarray
    x_fm = c(np.asarray(L0_attn_x, np.float32).reshape(-1, D).T)
    gTh = c(np.asarray(L0_attn_proj_matrix, np.float32).T)
    iwTh = c(np.asarray(L0_attn_input_weights, np.float32).T)
    xis = np.asarray(L0_attn_xis, np.float32)
    shared = dict(
        gT=gTh, iwT=iwTh,
        xisr=c(xis.reshape(1, R)),
        xispp=c(xis.reshape(NCH, 128).T),
        gampp=c(np.asarray(L0_ffn_ln_gamma, np.float32).reshape(NCH, 128).T),
        betpp=c(np.asarray(L0_ffn_ln_beta, np.float32).reshape(NCH, 128).T),
        b2pp=c(np.asarray(L0_ffn_fc2_bias, np.float32).reshape(NCH, 128).T),
        b1pp=c(np.asarray(L0_ffn_fc1_bias, np.float32).reshape(FCH, 128).T),
        q1=c(np.asarray(L0_ffn_fc1_q).astype(np.uint8)),
        q2=c(np.asarray(L0_ffn_fc2_q).astype(np.uint8)),
        s1c=c(np.asarray(L0_ffn_fc1_scales, np.float32)),
        s2c=c(np.asarray(L0_ffn_fc2_scales, np.float32)),
        sel8=_selector(8),
        sel32=_selector(32),
    )
    in_maps = []
    for k in range(NCORES):
        m = dict(shared)
        m["xk"] = c(x_fm[:, k * TCORE:(k + 1) * TCORE])
        in_maps.append(m)

    nc = _get_nc()
    res = run_bass_kernel_spmd(nc, in_maps, core_ids=list(range(NCORES)))
    out_fm = np.concatenate([res.results[k]["out"] for k in range(NCORES)], axis=1)
    return c(out_fm.T).reshape(2, 8192, D)
